# version 06
# baseline (speedup 1.0000x reference)
import numpy as np
import ml_dtypes
from contextlib import ExitStack

import concourse.bass as bass
import concourse.mybir as mybir
from concourse.bass_utils import run_bass_kernel_spmd

F32 = mybir.dt.float32
BF16 = mybir.dt.bfloat16
AF = mybir.ActivationFunctionType
ALU = mybir.AluOpType
AX = mybir.AxisListType

T = 4096
D = 1024
NTILE = 32
H = 16
HD = 64
E = 8
FF = 4096
NQ = 4
TQ = 1024
QC = 512
C_SHIFT = -32.0
EPS = 1e-5
QSCALE = 0.125
NEG = -1.0e30

TAPS = set()
PHASES = (1, 2, 3, 4)
P2_HEADS = H
P2_QC = 8
P2_DBG = 0

ENGS = ("tensor", "vector", "scalar", "gpsimd", "sync")


class Sched:

    def __init__(self):
        self.marks = {}
        self._c1 = {}
        self._c2 = {}
        self.counts = self._c1
        self.pass2 = False
        self.missing = set()

    def begin_pass(self, pass2):
        self.pass2 = pass2
        self.counts = self._c2 if pass2 else self._c1

    def inc(self, sem, n, label=None):
        c = self.counts.get(sem, 0) + n
        self.counts[sem] = c
        if label is not None:
            if self.pass2:
                assert self.marks.get(label) == c, (label, self.marks.get(label), c)
            else:
                assert label not in self.marks, f"dup label {label}"
                self.marks[label] = c
        return c

    def at(self, label):
        if label in self.marks:
            return self.marks[label]
        assert not self.pass2, f"unknown label {label}"
        self.missing.add(label)
        return 0


class Eng:

    def __init__(self, S, sems, eng=None):
        self.S = S
        self.sems = sems
        self.eng = eng

    def wait(self, sem, label):
        v = self.S.at(label)
        if self.eng is not None and v > 0:
            self.eng.wait_ge(self.sems[sem], v)

    def wait_val(self, sem, v):
        if self.eng is not None and v > 0:
            self.eng.wait_ge(self.sems[sem], v)

    def op(self, fname, *a, sem=None, n=1, label=None, **k):
        if self.eng is not None:
            inst = getattr(self.eng, fname)(*a, **k)
            if sem is not None:
                inst.then_inc(self.sems[sem], n)
        if sem is not None:
            self.S.inc(sem, n, label)


def _lp(nc, real):
    if real:
        return nc.allow_low_precision(reason="bf16 matmul operand rounding")
    import contextlib
    return contextlib.nullcontext()


def build_nc():
    nc = bass.Bass("TRN2", target_bir_lowering=False)

    x_d = nc.dram_tensor("x", [T, D], F32, kind="ExternalInput").ap()
    wq_d = nc.dram_tensor("wq", [D, D], F32, kind="ExternalInput").ap()
    wk_d = nc.dram_tensor("wk", [D, D], F32, kind="ExternalInput").ap()
    wv_d = nc.dram_tensor("wv", [D, D], F32, kind="ExternalInput").ap()
    wo_d = nc.dram_tensor("wo", [D, D], F32, kind="ExternalInput").ap()
    bo_d = nc.dram_tensor("bo", [1, D], F32, kind="ExternalInput").ap()
    gw_d = nc.dram_tensor("gw", [E, D], F32, kind="ExternalInput").ap()
    w1_d = nc.dram_tensor("w1", [E, D, FF], F32, kind="ExternalInput").ap()
    b1_d = nc.dram_tensor("b1", [E, FF], F32, kind="ExternalInput").ap()
    w2_d = nc.dram_tensor("w2", [E, FF, D], F32, kind="ExternalInput").ap()
    b2_d = nc.dram_tensor("b2", [E, D], F32, kind="ExternalInput").ap()
    idb_d = nc.dram_tensor("idb", [128, 128], BF16, kind="ExternalInput").ap()
    idf_d = nc.dram_tensor("idf", [128, 128], F32, kind="ExternalInput").ap()
    mask_d = nc.dram_tensor("maskc", [128, 896], F32, kind="ExternalInput").ap()
    ones_d = nc.dram_tensor("onesb", [1, 128], BF16, kind="ExternalInput").ap()

    y_d = nc.dram_tensor("y", [T, D], F32, kind="ExternalOutput").ap()

    def scratch(name, shape, dt, tap):
        kind = "ExternalOutput" if tap in TAPS else "Internal"
        return nc.dram_tensor(name, shape, dt, kind=kind).ap()

    w1b = nc.dram_tensor("w1b", [E, D, FF], BF16, kind="Internal").ap()
    w2b = nc.dram_tensor("w2b", [E, FF, D], BF16, kind="Internal").ap()
    qsp = scratch("qsp", [D, T], BF16, "qsp")
    ksp = scratch("ksp", [D, T], BF16, "ksp")
    x1sp = scratch("x1sp", [T, D], F32, "x1sp")
    h2sp = scratch("h2sp", [D, T], BF16, "h2sp")
    tap_h1T = scratch("tap_h1T", [128, 8 * T], BF16, "h1T") if "h1T" in TAPS else None
    tap_vaug = scratch("tap_vaug", [128, H * 32 * 65], BF16, "vaug") if "vaug" in TAPS else None
    tap_oT = scratch("tap_oT", [128, 8 * T], BF16, "oT") if "oT" in TAPS else None
    tap_g = scratch("tap_g", [128, NTILE * E], F32, "g") if "g" in TAPS else None

    S = Sched()
    es = ExitStack()

    sems = {}
    for s in ("d_x", "d_so", "d_w", "s_pe", "s_sa", "s_ve", "s_gp", "s_bar"):
        sems[s] = es.enter_context(nc.semaphore(s))

    idb = es.enter_context(nc.sbuf_tensor("idb_s", [128, 128], BF16))
    idf = es.enter_context(nc.sbuf_tensor("idf_s", [128, 128], F32))
    onesb = es.enter_context(nc.sbuf_tensor("ones_s", [1, 128], BF16))
    g_sb = es.enter_context(nc.sbuf_tensor("g_sb", [128, NTILE * E], F32))
    GT = es.enter_context(nc.sbuf_tensor("GT", [8, T], BF16))
    b2s = es.enter_context(nc.sbuf_tensor("b2s", [8, D], BF16))
    b1T = es.enter_context(nc.sbuf_tensor("b1T", [128, E * 32], F32))
    cm32 = es.enter_context(nc.sbuf_tensor("cm32", [128, 1], F32))

    es_r = ExitStack()
    vaug = es_r.enter_context(nc.sbuf_tensor("vaug", [128, H * 32 * 65], BF16, side="right"))

    barrier_n = [0]

    def run_phase(fns):
        S.begin_pass(False)
        for en in ENGS:
            if en in fns:
                fns[en](Eng(S, sems))
        missing = S.missing - set(S.marks)
        assert not missing, f"labels never defined: {sorted(missing)[:10]}"
        S.missing = set()
        S.begin_pass(True)
        with nc.Block() as blk:
            for en in ENGS:
                if en in fns:
                    getattr(blk, en)(lambda eng, fn=fns[en]: fn(Eng(S, sems, eng)))

    def barrier():
        barrier_n[0] += 1
        tgt = 5 * barrier_n[0]

        def mk(en):
            def f(e):
                e.op("nop", sem="s_bar", n=1)
                e.wait_val("s_bar", tgt)
            return f
        run_phase({en: mk(en) for en in ENGS})

    def p0_gpsimd(e):
        e.op("memset", cm32[:, :], C_SHIFT, sem="s_gp", n=1, label="c_m32")

    def p0_sync(e):
        e.op("dma_start", idb[:, :], idb_d[:, :], sem="d_x", n=16, label="c_idb")
        e.op("dma_start", idf[:, :], idf_d[:, :], sem="d_x", n=16, label="c_idf")
        e.op("dma_start", onesb[:, :], ones_d[:, :], sem="d_x", n=16, label="c_ones")

    run_phase({"sync": p0_sync, "gpsimd": p0_gpsimd})

    if 1 in PHASES:
        p1 = ExitStack()
        xt = [p1.enter_context(nc.sbuf_tensor(f"xt{i}", [128, D], F32)) for i in range(3)]
        h1s = [p1.enter_context(nc.sbuf_tensor(f"h1s{i}", [128, D], BF16)) for i in range(2)]
        sqd = p1.enter_context(nc.sbuf_tensor("sqd", [128, D], BF16))
        st = {}
        for nm in ("ssum", "ssq", "mm_", "vv", "m2v", "var0", "lnv", "rstd", "nb"):
            st[nm] = p1.enter_context(nc.sbuf_tensor(nm, [128, 2], F32))
        h1T = p1.enter_context(nc.sbuf_tensor("h1T", [128, 8 * T], BF16))
        wbuf = [p1.enter_context(nc.sbuf_tensor(f"wbuf{i}", [128, 8 * D], BF16))
                for i in range(2)]
        wqs, wks, wvs = wbuf[0], wbuf[1], wbuf[0]
        qks = [p1.enter_context(nc.sbuf_tensor(f"qks{i}", [128, 512], BF16)) for i in range(4)]
        pt = [p1.enter_context(nc.psum_tensor(f"pt{i}", [128, 1024], BF16)) for i in range(2)]
        psq = [p1.enter_context(nc.psum_tensor(f"psq{i}", [128, 512], F32)) for i in range(2)]
        psv = [p1.enter_context(nc.psum_tensor(f"psv{i}", [128, 512], F32)) for i in range(2)]

        def p1_gpsimd(e):
            for nm, sb, dr in (("q", wqs, wq_d), ("k", wks, wk_d)):
                src = dr.rearrange("(dc p) c -> p dc c", p=128)
                dst = sb[:, :].rearrange("p (dc c) -> p dc c", c=D)
                e.op("dma_start", dst, src, sem="d_w", n=16, label=f"p1_w{nm}")
            e.wait("s_pe", "p1_qk_0_7_7")
            src = wv_d.rearrange("(dc p) c -> p dc c", p=128)
            dst = wvs[:, :].rearrange("p (dc c) -> p dc c", c=D)
            e.op("dma_start", dst, src, sem="d_w", n=16, label="p1_wv")
            va = bass.AP(vaug, HD, [[vaug.shape[1], 128], [65, H * 32], [1, 1]])
            e.op("memset", va, 1.0, sem="s_gp", n=1, label="p1_vones")

        def p1_sync(e):
            for i in range(NTILE):
                if i >= 3:
                    e.wait("s_sa", f"p1_norm_{i-3}")
                e.op("dma_start", xt[i % 3][:, :], x_d[i * 128:(i + 1) * 128, :],
                     sem="d_x", n=16, label=f"p1_x_{i}")

        def p1_vector(e):
            inv_d = 1.0 / D
            for i in range(NTILE):
                p = i % 2
                e.wait("d_x", f"p1_x_{i}")
                e.op("tensor_reduce", st["ssum"][:, p:p + 1], xt[i % 3][:, :], AX.X, ALU.add,
                     sem="s_ve", n=1, label=f"p1_sum_{i}")
                e.wait("s_ve", f"p1_sum_{i}")
                e.op("tensor_scalar", st["mm_"][:, p:p + 1], st["ssum"][:, p:p + 1],
                     inv_d, 0.0, ALU.mult, ALU.bypass, sem="s_ve", n=1, label=f"p1_m_{i}")
                e.wait("s_ve", f"p1_m_{i}")
                e.wait("s_sa", f"p1_sq_{i}")
                e.op("tensor_scalar", st["vv"][:, p:p + 1], st["ssq"][:, p:p + 1],
                     inv_d, EPS, ALU.mult, ALU.add, sem="s_ve", n=1, label=f"p1_vv_{i}")
                e.op("tensor_tensor", st["m2v"][:, p:p + 1], st["mm_"][:, p:p + 1],
                     st["mm_"][:, p:p + 1], ALU.mult, sem="s_ve", n=1, label=f"p1_m2_{i}")
                e.wait("s_ve", f"p1_m2_{i}")
                e.op("tensor_tensor", st["var0"][:, p:p + 1], st["vv"][:, p:p + 1],
                     st["m2v"][:, p:p + 1], ALU.subtract, sem="s_ve", n=1, label=f"p1_var_{i}")
                e.wait("s_sa", f"p1_rstd_{i}")
                e.op("tensor_scalar", st["nb"][:, p:p + 1], st["mm_"][:, p:p + 1],
                     st["rstd"][:, p:p + 1], -1.0, ALU.mult, ALU.mult,
                     sem="s_ve", n=1, label=f"p1_nb_{i}")
                if i > 0:
                    for dc in range(8):
                        j = (i - 1) * 8 + dc
                        e.wait("s_pe", f"p1_tr_{i-1}_{dc}")
                        e.op("tensor_copy", h1T[:, dc * T + (i - 1) * 128: dc * T + i * 128],
                             pt[j % 2][:, 0:128], sem="s_ve", n=1, label=f"p1_te_{i-1}_{dc}")
            for dc in range(8):
                j = (NTILE - 1) * 8 + dc
                e.wait("s_pe", f"p1_tr_{NTILE-1}_{dc}")
                e.op("tensor_copy", h1T[:, dc * T + (NTILE - 1) * 128: dc * T + NTILE * 128],
                     pt[j % 2][:, 0:128], sem="s_ve", n=1, label=f"p1_te_{NTILE-1}_{dc}")
            for tt in range(NTILE):
                for dh in range(2):
                    j = tt * 2 + dh
                    e.wait("s_pe", f"p1_v_{tt}_{dh}")
                    dst = bass.AP(vaug, ((dh * 8) * 32 + tt) * 65,
                                  [[vaug.shape[1], 128], [32 * 65, 8], [1, 64]])
                    src = psv[j % 2][:, :].rearrange("p (h c) -> p h c", c=64)
                    with _lp(nc, e.eng is not None):
                        e.op("tensor_copy", dst, src, sem="s_ve", n=1, label=f"p1_ve_{tt}_{dh}")

        def p1_scalar(e):
            for i in range(NTILE):
                p = i % 2
                e.wait("d_x", f"p1_x_{i}")
                if i >= 2:
                    e.wait("s_sa", f"p1_sq_{i-2}")
                    e.wait("s_ve", f"p1_vv_{i-2}")
                with _lp(nc, e.eng is not None):
                    e.op("activation", sqd[:, :], xt[i % 3][:, :], AF.Square,
                         accum_out=st["ssq"][:, p:p + 1], sem="s_sa", n=1, label=f"p1_sq_{i}")
                e.wait("s_ve", f"p1_var_{i}")
                e.op("activation", st["lnv"][:, p:p + 1], st["var0"][:, p:p + 1], AF.Ln,
                     sem="s_sa", n=1, label=f"p1_ln_{i}")
                e.wait("s_sa", f"p1_ln_{i}")
                e.op("activation", st["rstd"][:, p:p + 1], st["lnv"][:, p:p + 1], AF.Exp,
                     scale=-0.5, sem="s_sa", n=1, label=f"p1_rstd_{i}")
                e.wait("s_sa", f"p1_rstd_{i}")
                e.wait("s_ve", f"p1_nb_{i}")
                if i >= 2:
                    e.wait("s_pe", f"p1_tr_{i-2}_7")
                with _lp(nc, e.eng is not None):
                    e.op("activation", h1s[p][:, :], xt[i % 3][:, :], AF.Identity,
                         bias=st["nb"][:, p:p + 1], scale=st["rstd"][:, p:p + 1],
                         sem="s_sa", n=1, label=f"p1_norm_{i}")
            for kk, (sc, spd) in enumerate(((QSCALE, qsp), (1.0, ksp))):
                for dt in range(8):
                    for tc in range(8):
                        j = (kk * 8 + dt) * 8 + tc
                        e.wait("s_pe", f"p1_qk_{kk}_{dt}_{tc}")
                        if j >= 4:
                            e.wait("d_so", f"p1_qkd_{j-4}")
                        with _lp(nc, e.eng is not None):
                            e.op("activation", qks[j % 4][:, :], psq[j % 2][:, :], AF.Copy,
                                 scale=sc, sem="s_sa", n=1, label=f"p1_qke_{j}")
                        e.wait("s_sa", f"p1_qke_{j}")
                        e.op("dma_start", spd[dt * 128:(dt + 1) * 128, tc * 512:(tc + 1) * 512],
                             qks[j % 4][:, :], sem="d_so", n=16, label=f"p1_qkd_{j}")
            e.wait("d_so", f"p1_qkd_{127}")

        def p1_tensor(e):
            e.wait("d_x", "c_idb")
            for i in range(NTILE):
                e.wait("s_sa", f"p1_norm_{i}")
                for dc in range(8):
                    j = i * 8 + dc
                    if j >= 2:
                        e.wait("s_ve", f"p1_te_{(j-2)//8}_{(j-2)%8}")
                    e.op("transpose", pt[j % 2][:, 0:128], h1s[i % 2][:, dc * 128:(dc + 1) * 128],
                         idb[:, :], sem="s_pe", n=1, label=f"p1_tr_{i}_{dc}")
            e.wait("d_w", "p1_wq")
            e.wait("d_w", "p1_wk")
            for kk, wsb in enumerate((wqs, wks)):
                for dt in range(8):
                    for tc in range(8):
                        j = (kk * 8 + dt) * 8 + tc
                        if j >= 2:
                            e.wait("s_sa", f"p1_qke_{j-2}")
                        for dc in range(8):
                            e.op("matmul", psq[j % 2][:, :],
                                 wsb[:, dc * D + dt * 128: dc * D + (dt + 1) * 128],
                                 h1T[:, dc * T + tc * 512: dc * T + (tc + 1) * 512],
                                 start=(dc == 0), stop=(dc == 7),
                                 sem=("s_pe" if dc == 7 else None), n=1,
                                 label=(f"p1_qk_{kk}_{dt}_{tc}" if dc == 7 else None))
            e.wait("s_gp", "p1_vones")
            e.wait("d_w", "p1_wv")
            for tt in range(NTILE):
                for dh in range(2):
                    j = tt * 2 + dh
                    if j >= 2:
                        e.wait("s_ve", f"p1_ve_{(j-2)//2}_{(j-2)%2}")
                    for dc in range(8):
                        e.op("matmul", psv[j % 2][:, :],
                             h1T[:, dc * T + tt * 128: dc * T + (tt + 1) * 128],
                             wvs[:, dc * D + dh * 512: dc * D + (dh + 1) * 512],
                             start=(dc == 0), stop=(dc == 7),
                             sem=("s_pe" if dc == 7 else None), n=1,
                             label=(f"p1_v_{tt}_{dh}" if dc == 7 else None))

        run_phase({"gpsimd": p1_gpsimd, "sync": p1_sync, "vector": p1_vector,
                   "scalar": p1_scalar, "tensor": p1_tensor})

        if "h1T" in TAPS or "vaug" in TAPS:
            def p1_tap(e):
                e.wait("s_ve", f"p1_te_{NTILE-1}_7")
                e.wait("s_ve", f"p1_ve_{NTILE-1}_1")
                last = None
                if tap_h1T is not None:
                    e.op("dma_start", tap_h1T[:, :], h1T[:, :], sem="d_x", n=16,
                         label="p1_tapd1")
                    last = "p1_tapd1"
                if tap_vaug is not None:
                    e.op("dma_start", tap_vaug[:, :], vaug[:, :], sem="d_x", n=16,
                         label="p1_tapd2")
                    last = "p1_tapd2"
                e.wait("d_x", last)
            run_phase({"sync": p1_tap})

        barrier()
        p1.close()

    oT = es_r.enter_context(nc.sbuf_tensor("oT", [128, 8 * T], BF16, side="right"))
    if 2 in PHASES:
        p2 = ExitStack()
        qh2 = [p2.enter_context(nc.sbuf_tensor(f"qh2{i}", [128, T], BF16)) for i in range(2)]
        kh2 = [p2.enter_context(nc.sbuf_tensor(f"kh2{i}", [128, T], BF16)) for i in range(2)]
        pbuf = [p2.enter_context(nc.sbuf_tensor(f"pbuf{i}", [128, 512], BF16)) for i in range(4)]
        stmp = [p2.enter_context(nc.sbuf_tensor(f"stmp{i}", [128, 512], F32)) for i in range(2)]
        masks = p2.enter_context(nc.sbuf_tensor("masks", [128, 896], F32))
        linvf = [p2.enter_context(nc.sbuf_tensor(f"linvf{i}", [1, 512], F32)) for i in range(2)]
        linvb = [p2.enter_context(nc.sbuf_tensor(f"linvb{i}", [1, 512], BF16)) for i in range(2)]
        linvs = [p2.enter_context(nc.sbuf_tensor(f"linvs{i}", [64, 512], F32)) for i in range(2)]
        pcs = [p2.enter_context(nc.sbuf_tensor(f"pcs{i}", [128, 2048], BF16)) for i in range(2)]
        ps_s = [p2.enter_context(nc.psum_tensor(f"ps_s{i}", [128, 512], F32)) for i in range(4)]
        ps_o = [p2.enter_context(nc.psum_tensor(f"ps_o{i}", [65, 512], F32)) for i in range(2)]
        ps_l = [p2.enter_context(nc.psum_tensor(f"ps_l{i}", [64, 512], F32)) for i in range(2)]
        NHP = P2_HEADS // 2

        def p2_iter():
            gu = 0
            gm = 0
            for hp in range(NHP):
                for qc in range(P2_QC):
                    nstrip = 4 * (qc + 1)
                    units = []
                    for kt in range(nstrip):
                        masked = kt >= nstrip - 4
                        ua = (kt, 0, gu, gm if masked else None, masked)
                        gm += 1 if masked else 0
                        gu += 1
                        ub = (kt, 1, gu, (gm if masked else None), masked)
                        gm += 1 if masked else 0
                        gu += 1
                        if masked:
                            ua = (kt, 0, ua[2], ua[3], True)
                            ub = (kt, 1, ub[2], ub[3], True)
                        units.append((ua, ub))
                    yield hp, qc, nstrip, units

        P2IT = list(p2_iter())
        GU_OF_M = {}
        for hp, qc, nstrip, units in P2IT:
            for ua, ub in units:
                for (kt, sd_, gu, gm, masked) in (ua, ub):
                    if masked:
                        GU_OF_M[gm] = gu

        def p2_gpsimd(e):
            j = 0
            for ee in range(E):
                for blk in range(16):
                    if j >= 2:
                        e.wait("d_w", f"pc_w_{j-2}")
                    r0, fh = (blk // 2) * 128, (blk % 2) * 2048
                    e.op("dma_start", pcs[j % 2][:, :],
                         w1_d[ee, r0:r0 + 128, fh:fh + 2048],
                         max_dma_last_dim=8192, sem="d_w", n=16, label=f"pc_r_{j}")
                    e.wait("d_w", f"pc_r_{j}")
                    e.op("dma_start", w1b[ee, r0:r0 + 128, fh:fh + 2048],
                         pcs[j % 2][:, :], sem="d_w", n=16, label=f"pc_w_{j}")
                    j += 1
                for blk in range(16):
                    if j >= 2:
                        e.wait("d_w", f"pc_w_{j-2}")
                    src = bass.AP(w2_d.tensor, ee * FF * D + blk * 256 * D,
                                  [[D, 128], [128 * D, 2], [1, D]])
                    e.op("dma_start", pcs[j % 2][:, :].rearrange("p (r c) -> p r c", c=D),
                         src, max_dma_last_dim=8192, sem="d_w", n=16, label=f"pc_r_{j}")
                    e.wait("d_w", f"pc_r_{j}")
                    dst = bass.AP(w2b.tensor, ee * FF * D + blk * 256 * D,
                                  [[D, 128], [128 * D, 2], [1, D]])
                    e.op("dma_start", dst, pcs[j % 2][:, :].rearrange("p (r c) -> p r c", c=D),
                         sem="d_w", n=16, label=f"pc_w_{j}")
                    j += 1

        def p2_sync(e):
            e.op("dma_start", masks[:, :], mask_d[:, :], sem="d_x", n=16, label="p2_mask")
            last_pv = {}
            for hp, qc, nstrip, units in P2IT:
                last_pv[hp] = units[-1][1][2]
            for hp in range(NHP):
                if hp >= 2:
                    e.wait("s_pe", f"p2_pv_{last_pv[hp-2]}")
                e.op("dma_start", qh2[hp % 2][:, :], qsp[hp * 128:(hp + 1) * 128, :],
                     sem="d_x", n=16, label=f"p2_q_{hp}")
                e.op("dma_start", kh2[hp % 2][:, :], ksp[hp * 128:(hp + 1) * 128, :],
                     sem="d_x", n=16, label=f"p2_k_{hp}")

        def p2_tensor(e):
            pend = None
            for hp, qc, nstrip, units in P2IT:
                idx = hp * 8 + qc
                e.wait("d_x", f"p2_q_{hp}")
                e.wait("d_x", f"p2_k_{hp}")

                def emit_s(kt):
                    ua, ub = units[kt]
                    for (ktx, sd_, gu, gm, masked) in (ua, ub):
                        if gu >= 4:
                            e.wait("s_sa", f"p2_exp_{gu-4}")
                    for (ktx, sd_, gu, gm, masked) in (ua, ub):
                        e.op("matmul", ps_s[sd_ * 2 + (kt % 2)][:, :],
                             kh2[hp % 2][sd_ * 64:sd_ * 64 + 64, kt * 128:(kt + 1) * 128],
                             qh2[hp % 2][sd_ * 64:sd_ * 64 + 64, qc * 512:(qc + 1) * 512],
                             start=True, stop=True,
                             tile_position=(None if P2_DBG == 2 else (sd_ * 64, 0)),
                             sem="s_pe", n=1, label=f"p2_s_{gu}")

                def emit_pv(kt):
                    ua, ub = units[kt]
                    for (ktx, sd_, gu, gm, masked) in (ua, ub):
                        e.wait("s_sa", f"p2_exp_{gu}")
                        h = 2 * hp + sd_
                        st_ = True if P2_DBG == 1 else (kt == 0)
                        sp_ = True if P2_DBG == 1 else (kt == nstrip - 1)
                        e.op("matmul", ps_o[sd_][:, :],
                             vaug[:, (h * 32 + kt) * 65: (h * 32 + kt) * 65 + 65],
                             pbuf[sd_ * 2 + (kt % 2)][:, :],
                             start=st_, stop=sp_,
                             sem="s_pe", n=1, label=f"p2_pv_{gu}")

                LEAD = 2
                lead = min(LEAD, nstrip)
                for j in range(lead):
                    emit_s(j)
                if pend is not None:
                    pidx = pend
                    for sd_ in range(2):
                        e.wait("s_ve", f"p2_linv_{pidx}_{sd_}")
                        if pidx >= 1:
                            e.wait("s_sa", f"p2_lc_{pidx-1}_{sd_}")
                        e.op("matmul", ps_l[sd_][:, :], onesb[0:1, 0:64],
                             linvb[sd_][0:1, :], start=True, stop=True,
                             sem="s_pe", n=1, label=f"p2_lb_{pidx}_{sd_}")
                    pend = None
                if idx >= 1:
                    e.wait("s_ve", f"p2_otev_{idx-1}_0")
                    e.wait("s_ve", f"p2_otev_{idx-1}_1")
                for j in range(lead, nstrip):
                    emit_s(j)
                    emit_pv(j - lead)
                for j in range(nstrip - lead, nstrip):
                    emit_pv(j)
                pend = idx
            pidx = pend
            for sd_ in range(2):
                e.wait("s_ve", f"p2_linv_{pidx}_{sd_}")
                if pidx >= 1:
                    e.wait("s_sa", f"p2_lc_{pidx-1}_{sd_}")
                e.op("matmul", ps_l[sd_][:, :], onesb[0:1, 0:64], linvb[sd_][0:1, :],
                     start=True, stop=True, sem="s_pe", n=1, label=f"p2_lb_{pidx}_{sd_}")

        def p2_vector(e):
            e.wait("d_x", "p2_mask")
            for hp, qc, nstrip, units in P2IT:
                idx = hp * 8 + qc
                if idx >= 1:
                    pidx = idx - 1
                    php, pqc = pidx // 8, pidx % 8
                    for sd_ in range(2):
                        ph_ = 2 * php + sd_
                        e.wait("s_sa", f"p2_lc_{pidx}_{sd_}")
                        dst = oT[sd_ * 64:sd_ * 64 + 64,
                                 (ph_ // 2) * T + pqc * 512:(ph_ // 2) * T + (pqc + 1) * 512]
                        with _lp(nc, e.eng is not None):
                            e.op("tensor_tensor", dst, ps_o[sd_][0:64, :], linvs[sd_][:, :],
                                 ALU.mult, sem="s_ve", n=1, label=f"p2_otev_{pidx}_{sd_}")
                for ua, ub in units:
                    for (kt, sd_, gu, gm, masked) in (ua, ub):
                        if not masked:
                            continue
                        e.wait("s_pe", f"p2_s_{gu}")
                        if gm >= 2:
                            e.wait("s_sa", f"p2_exp_{GU_OF_M[gm-2]}")
                        i_rel = kt - 4 * qc
                        e.op("tensor_tensor", stmp[gm % 2][:, :],
                             ps_s[sd_ * 2 + (kt % 2)][:, :],
                             masks[:, 384 - 128 * i_rel: 896 - 128 * i_rel],
                             ALU.add, sem="s_ve", n=1, label=f"p2_m_{gu}")
                for sd_ in range(2):
                    e.wait("s_pe", f"p2_pv_{units[-1][sd_][2]}")
                    e.op("reciprocal", linvf[sd_][:, :], ps_o[sd_][64:65, :],
                         sem="s_ve", n=1, label=f"p2_rc_{idx}_{sd_}")
                    e.wait("s_ve", f"p2_rc_{idx}_{sd_}")
                    if idx >= 1:
                        e.wait("s_pe", f"p2_lb_{idx-1}_{sd_}")
                    with _lp(nc, e.eng is not None):
                        e.op("tensor_copy", linvb[sd_][:, :], linvf[sd_][:, :],
                             sem="s_ve", n=1, label=f"p2_linv_{idx}_{sd_}")
            pidx = NHP * 8 - 1 if P2_QC == 8 else (NHP - 1) * 8 + (P2_QC - 1)
            php, pqc = pidx // 8, pidx % 8
            for sd_ in range(2):
                ph_ = 2 * php + sd_
                e.wait("s_sa", f"p2_lc_{pidx}_{sd_}")
                dst = oT[sd_ * 64:sd_ * 64 + 64,
                         (ph_ // 2) * T + pqc * 512:(ph_ // 2) * T + (pqc + 1) * 512]
                with _lp(nc, e.eng is not None):
                    e.op("tensor_tensor", dst, ps_o[sd_][0:64, :], linvs[sd_][:, :],
                         ALU.mult, sem="s_ve", n=1, label=f"p2_otev_{pidx}_{sd_}")

        def p2_scalar(e):
            for hp, qc, nstrip, units in P2IT:
                idx = hp * 8 + qc
                if idx >= 1:
                    pidx = idx - 1
                    for sd_ in range(2):
                        e.wait("s_pe", f"p2_lb_{pidx}_{sd_}")
                        if pidx >= 1:
                            e.wait("s_ve", f"p2_otev_{pidx-1}_{sd_}")
                        e.op("copy", linvs[sd_][:, :], ps_l[sd_][:, :],
                             sem="s_sa", n=1, label=f"p2_lc_{pidx}_{sd_}")
                for ua, ub in units:
                    for (kt, sd_, gu, gm, masked) in (ua, ub):
                        if gu >= 4:
                            e.wait("s_pe", f"p2_pv_{gu-4}")
                        if masked:
                            e.wait("s_ve", f"p2_m_{gu}")
                            src = stmp[gm % 2][:, :]
                        else:
                            e.wait("s_pe", f"p2_s_{gu}")
                            src = ps_s[sd_ * 2 + (kt % 2)][:, :]
                        if gu == 0:
                            e.wait("s_gp", "c_m32")
                        with _lp(nc, e.eng is not None):
                            e.op("activation", pbuf[sd_ * 2 + (kt % 2)][:, :], src, AF.Exp,
                                 bias=cm32[:, 0:1], sem="s_sa", n=1, label=f"p2_exp_{gu}")
            pidx = (NHP - 1) * 8 + (P2_QC - 1)
            for sd_ in range(2):
                e.wait("s_pe", f"p2_lb_{pidx}_{sd_}")
                if pidx >= 1:
                    e.wait("s_ve", f"p2_otev_{pidx-1}_{sd_}")
                e.op("copy", linvs[sd_][:, :], ps_l[sd_][:, :],
                     sem="s_sa", n=1, label=f"p2_lc_{pidx}_{sd_}")

        run_phase({"sync": p2_sync, "tensor": p2_tensor, "vector": p2_vector,
                   "scalar": p2_scalar, "gpsimd": p2_gpsimd})

        if "oT" in TAPS:
            def p2_tap(e):
                pidx = (P2_HEADS // 2 - 1) * 8 + (P2_QC - 1)
                e.wait("s_ve", f"p2_otev_{pidx}_0")
                e.wait("s_ve", f"p2_otev_{pidx}_1")
                e.op("dma_start", tap_oT[:, :], oT[:, :], sem="d_x", n=16)
            run_phase({"sync": p2_tap})

        barrier()
        p2.close()

    if 3 in PHASES:
        p3 = ExitStack()
        wos = p3.enter_context(nc.sbuf_tensor("wos", [128, 8 * D], BF16))
        bob = p3.enter_context(nc.sbuf_tensor("bob", [128, D], F32))
        xt3 = [p3.enter_context(nc.sbuf_tensor(f"xt3{i}", [128, D], F32)) for i in range(3)]
        x1t = [p3.enter_context(nc.sbuf_tensor(f"x1t{i}", [128, D], F32)) for i in range(2)]
        h2s = [p3.enter_context(nc.sbuf_tensor(f"h2s{i}", [128, D], BF16)) for i in range(2)]
        h2Ts = [p3.enter_context(nc.sbuf_tensor(f"h2Ts{i}", [128, D], BF16)) for i in range(2)]
        sqd3 = p3.enter_context(nc.sbuf_tensor("sqd3", [128, D], BF16))
        st3 = {}
        for nm in ("ssum", "ssq", "mm_", "vv", "m2v", "var0", "lnv", "rstd", "nb"):
            st3[nm] = p3.enter_context(nc.sbuf_tensor("g_" + nm, [128, 2], F32))
        gws = p3.enter_context(nc.sbuf_tensor("gws", [8, D], F32))
        gwT = p3.enter_context(nc.sbuf_tensor("gwT", [128, 64], BF16))
        lgs = p3.enter_context(nc.sbuf_tensor("lgs", [128, NTILE * E], F32))
        b1stg = p3.enter_context(nc.sbuf_tensor("b1stg", [32, 128], F32))
        gtmp = {}
        for nm in ("m1", "se", "ri", "m1p", "m2", ):
            gtmp[nm] = p3.enter_context(nc.sbuf_tensor("t_" + nm, [128, 1], F32))
        for nm in ("ls", "pe_", "pr_", "mk", "pm", "kp"):
            gtmp[nm] = p3.enter_context(nc.sbuf_tensor("t_" + nm, [128, 8], F32))
        ps_ao = [p3.enter_context(nc.psum_tensor(f"ps_ao{i}", [128, 1024], F32)) for i in range(2)]
        pt3 = [p3.enter_context(nc.psum_tensor(f"pt3{i}", [128, 1024], BF16)) for i in range(2)]
        ps_lg = [p3.enter_context(nc.psum_tensor(f"ps_lg{i}", [128, 512], F32)) for i in range(2)]

        def p3_gpsimd(e):
            src = wo_d.rearrange("(dc p) c -> p dc c", p=128)
            dst = wos[:, :].rearrange("p (dc c) -> p dc c", c=D)
            e.op("dma_start", dst, src, sem="d_w", n=16, label="p3_wo")
            e.op("dma_start", b2s[:, :], b2_d[:, :], sem="d_w", n=16, label="p3_b2")

        def p3_sync(e):
            e.op("dma_start", bob[:, :], bass.AP(bo_d.tensor, 0, [[0, 128], [1, D]]),
                 sem="d_x", n=16, label="p3_bo")
            e.op("dma_start", gws[:, :], gw_d[:, :], sem="d_x", n=16, label="p3_gw")
            b1r = b1_d.rearrange("e (r c) -> e r c", c=128)
            for ee in range(E):
                if ee >= 1:
                    e.wait("s_pe", f"p3_b1t_{ee-1}")
                e.op("dma_start", b1stg[:, :], b1r[ee], sem="d_x", n=16, label=f"p3_b1_{ee}")
            for i in range(NTILE):
                if i >= 3:
                    e.wait("s_ve", f"p3_x1_{i-3}")
                e.op("dma_start", xt3[i % 3][:, :], x_d[i * 128:(i + 1) * 128, :],
                     sem="d_x", n=16, label=f"p3_x_{i}")

        def p3_tensor(e):
            e.wait("d_x", "p3_gw")
            e.wait("d_x", "c_idf")
            for dc in range(8):
                if dc >= 2:
                    e.wait("s_sa", f"p3_gwe_{dc-2}")
                e.op("transpose", ps_lg[dc % 2][0:128, 0:8], gws[0:8, dc * 128:(dc + 1) * 128],
                     idf[0:8, 0:8], sem="s_pe", n=1, label=f"p3_gwt_{dc}")
            for ee in range(E):
                e.wait("d_x", f"p3_b1_{ee}")
                j = 8 + ee
                if ee >= 2:
                    e.wait("s_ve", f"p3_b1e_{ee-2}")
                else:
                    e.wait("s_sa", f"p3_gwe_{6+ee}")
                e.op("transpose", ps_lg[j % 2][0:128, 0:32], b1stg[:, :], idf[0:32, 0:32],
                     sem="s_pe", n=1, label=f"p3_b1t_{ee}")
            for i in range(NTILE):
                e.wait("d_w", "p3_wo")
                if i >= 2:
                    e.wait("s_ve", f"p3_x1_{i-2}")
                for dc in range(8):
                    for dh in range(2):
                        e.op("matmul", ps_ao[i % 2][:, dh * 512:(dh + 1) * 512],
                             oT[:, dc * T + i * 128: dc * T + (i + 1) * 128],
                             wos[:, dc * D + dh * 512: dc * D + (dh + 1) * 512],
                             start=(dc == 0), stop=(dc == 7),
                             sem=("s_pe" if (dc == 7 and dh == 1) else None), n=1,
                             label=(f"p3_ao_{i}" if (dc == 7 and dh == 1) else None))
                e.wait("s_sa", f"p3_norm_{i}")
                for dc in range(8):
                    j = i * 8 + dc
                    if j >= 2:
                        e.wait("s_ve", f"p3_te_{(j-2)//8}_{(j-2)%8}")
                    e.op("transpose", pt3[j % 2][:, 0:128], h2s[i % 2][:, dc * 128:(dc + 1) * 128],
                         idb[:, :], sem="s_pe", n=1, label=f"p3_tr_{i}_{dc}")
                e.wait("s_ve", f"p3_te_{i}_7")
                e.wait("s_sa", "p3_gwe_7")
                if i >= 2:
                    e.wait("s_ve", f"p3_lge_{i-2}")
                else:
                    e.wait("s_ve", f"p3_b1e_{6+i}")
                for dc in range(8):
                    e.op("matmul", ps_lg[i % 2][0:128, 0:8],
                         h2Ts[i % 2][:, dc * 128:(dc + 1) * 128], gwT[:, dc * 8:(dc + 1) * 8],
                         start=(dc == 0), stop=(dc == 7),
                         sem=("s_pe" if dc == 7 else None), n=1,
                         label=(f"p3_lg_{i}" if dc == 7 else None))
            for t in range(NTILE):
                e.wait("s_ve", f"p3_g_{t}")
                j = t % 2
                if t >= 2:
                    e.wait("s_ve", f"p3_gte_{t-2}")
                else:
                    e.wait("s_ve", f"p3_lge_{30+t}")
                e.op("transpose", ps_lg[j][0:8, 0:128], g_sb[:, t * 8:(t + 1) * 8], idf[:, :],
                     sem="s_pe", n=1, label=f"p3_gt_{t}")

        def p3_vector(e):
            inv_d = 1.0 / D
            for ee in range(E):
                e.wait("s_pe", f"p3_b1t_{ee}")
                e.op("tensor_copy", b1T[:, ee * 32:(ee + 1) * 32],
                     ps_lg[(8 + ee) % 2][0:128, 0:32], sem="s_ve", n=1, label=f"p3_b1e_{ee}")
            for i in range(NTILE):
                p = i % 2
                e.wait("s_pe", f"p3_ao_{i}")
                e.wait("d_x", f"p3_x_{i}")
                e.wait("d_x", "p3_bo")
                if i >= 2:
                    e.wait("d_so", f"p3_x1d_{i-2}")
                    e.wait("s_sa", f"p3_norm_{i-2}")
                e.op("tensor_tensor", x1t[p][:, :], ps_ao[p][:, :], bob[:, :], ALU.add,
                     sem="s_ve", n=1, label=f"p3_x1a_{i}")
                e.wait("s_ve", f"p3_x1a_{i}")
                e.op("tensor_tensor", x1t[p][:, :], x1t[p][:, :], xt3[i % 3][:, :], ALU.add,
                     sem="s_ve", n=1, label=f"p3_x1_{i}")
                e.wait("s_ve", f"p3_x1_{i}")
                e.op("tensor_reduce", st3["ssum"][:, p:p + 1], x1t[p][:, :], AX.X, ALU.add,
                     sem="s_ve", n=1, label=f"p3_sum_{i}")
                e.wait("s_ve", f"p3_sum_{i}")
                e.op("tensor_scalar", st3["mm_"][:, p:p + 1], st3["ssum"][:, p:p + 1],
                     inv_d, 0.0, ALU.mult, ALU.bypass, sem="s_ve", n=1, label=f"p3_m_{i}")
                e.wait("s_ve", f"p3_m_{i}")
                e.wait("s_sa", f"p3_sq_{i}")
                e.op("tensor_scalar", st3["vv"][:, p:p + 1], st3["ssq"][:, p:p + 1],
                     inv_d, EPS, ALU.mult, ALU.add, sem="s_ve", n=1, label=f"p3_vv_{i}")
                e.op("tensor_tensor", st3["m2v"][:, p:p + 1], st3["mm_"][:, p:p + 1],
                     st3["mm_"][:, p:p + 1], ALU.mult, sem="s_ve", n=1, label=f"p3_m2_{i}")
                e.wait("s_ve", f"p3_m2_{i}")
                e.op("tensor_tensor", st3["var0"][:, p:p + 1], st3["vv"][:, p:p + 1],
                     st3["m2v"][:, p:p + 1], ALU.subtract, sem="s_ve", n=1, label=f"p3_var_{i}")
                e.wait("s_sa", f"p3_rstd_{i}")
                e.op("tensor_scalar", st3["nb"][:, p:p + 1], st3["mm_"][:, p:p + 1],
                     st3["rstd"][:, p:p + 1], -1.0, ALU.mult, ALU.mult,
                     sem="s_ve", n=1, label=f"p3_nb_{i}")
                for dc in range(8):
                    j = i * 8 + dc
                    e.wait("s_pe", f"p3_tr_{i}_{dc}")
                    if i >= 2 and dc == 0:
                        e.wait("d_so", f"p3_h2d_{i-2}")
                        e.wait("s_pe", f"p3_lg_{i-2}")
                    e.op("tensor_copy", h2Ts[i % 2][:, dc * 128:(dc + 1) * 128],
                         pt3[j % 2][:, 0:128], sem="s_ve", n=1, label=f"p3_te_{i}_{dc}")
                e.wait("s_pe", f"p3_lg_{i}")
                e.op("tensor_copy", lgs[:, i * 8:(i + 1) * 8], ps_lg[i % 2][0:128, 0:8],
                     sem="s_ve", n=1, label=f"p3_lge_{i}")
            for t in range(NTILE):
                lg = lgs[:, t * 8:(t + 1) * 8]
                e.op("tensor_reduce", gtmp["m1"][:, :], lg, AX.X, ALU.max,
                     sem="s_ve", n=1, label=f"p3_ga_{t}")
                e.wait("s_ve", f"p3_ga_{t}")
                if t >= 1:
                    e.wait("s_sa", f"p3_ge_{t-1}")
                e.op("tensor_scalar", gtmp["ls"][:, :], lg, gtmp["m1"][:, 0:1], 0.0,
                     ALU.subtract, ALU.bypass, sem="s_ve", n=1, label=f"p3_gb_{t}")
                e.wait("s_sa", f"p3_ge_{t}")
                e.op("tensor_reduce", gtmp["se"][:, :], gtmp["pe_"][:, :], AX.X, ALU.add,
                     sem="s_ve", n=1, label=f"p3_gc_{t}")
                e.wait("s_ve", f"p3_gc_{t}")
                e.op("reciprocal", gtmp["ri"][:, :], gtmp["se"][:, :],
                     sem="s_ve", n=1, label=f"p3_gd_{t}")
                e.wait("s_ve", f"p3_gd_{t}")
                e.op("tensor_scalar", gtmp["pr_"][:, :], gtmp["pe_"][:, :],
                     gtmp["ri"][:, 0:1], 0.0, ALU.mult, ALU.bypass,
                     sem="s_ve", n=1, label=f"p3_gf_{t}")
                e.wait("s_ve", f"p3_gf_{t}")
                e.op("tensor_reduce", gtmp["m1p"][:, :], gtmp["pr_"][:, :], AX.X, ALU.max,
                     sem="s_ve", n=1, label=f"p3_gg_{t}")
                e.wait("s_ve", f"p3_gg_{t}")
                e.op("tensor_scalar", gtmp["mk"][:, :], gtmp["pr_"][:, :],
                     gtmp["m1p"][:, 0:1], NEG, ALU.is_ge, ALU.mult,
                     sem="s_ve", n=1, label=f"p3_gh_{t}")
                e.wait("s_ve", f"p3_gh_{t}")
                e.op("tensor_tensor", gtmp["pm"][:, :], gtmp["pr_"][:, :], gtmp["mk"][:, :],
                     ALU.add, sem="s_ve", n=1, label=f"p3_gi_{t}")
                e.wait("s_ve", f"p3_gi_{t}")
                e.op("tensor_reduce", gtmp["m2"][:, :], gtmp["pm"][:, :], AX.X, ALU.max,
                     sem="s_ve", n=1, label=f"p3_gj_{t}")
                e.wait("s_ve", f"p3_gj_{t}")
                e.op("tensor_scalar", gtmp["kp"][:, :], gtmp["pr_"][:, :],
                     gtmp["m2"][:, 0:1], 0.0, ALU.is_ge, ALU.bypass,
                     sem="s_ve", n=1, label=f"p3_gk_{t}")
                e.wait("s_ve", f"p3_gk_{t}")
                e.op("tensor_tensor", g_sb[:, t * 8:(t + 1) * 8], gtmp["pr_"][:, :],
                     gtmp["kp"][:, :], ALU.mult, sem="s_ve", n=1, label=f"p3_g_{t}")
            for t in range(NTILE):
                e.wait("s_pe", f"p3_gt_{t}")
                with _lp(nc, e.eng is not None):
                    e.op("tensor_copy", GT[0:8, t * 128:(t + 1) * 128],
                         ps_lg[t % 2][0:8, 0:128], sem="s_ve", n=1, label=f"p3_gte_{t}")

        def p3_scalar(e):
            for dc in range(8):
                e.wait("s_pe", f"p3_gwt_{dc}")
                with _lp(nc, e.eng is not None):
                    e.op("copy", gwT[:, dc * 8:(dc + 1) * 8], ps_lg[dc % 2][0:128, 0:8],
                         sem="s_sa", n=1, label=f"p3_gwe_{dc}")
            for i in range(NTILE):
                p = i % 2
                e.wait("s_ve", f"p3_x1_{i}")
                if i >= 2:
                    e.wait("s_ve", f"p3_vv_{i-2}")
                with _lp(nc, e.eng is not None):
                    e.op("activation", sqd3[:, :], x1t[p][:, :], AF.Square,
                         accum_out=st3["ssq"][:, p:p + 1], sem="s_sa", n=1, label=f"p3_sq_{i}")
                e.op("dma_start", x1sp[i * 128:(i + 1) * 128, :], x1t[p][:, :],
                     sem="d_so", n=16, label=f"p3_x1d_{i}")
                e.wait("s_ve", f"p3_var_{i}")
                e.op("activation", st3["lnv"][:, p:p + 1], st3["var0"][:, p:p + 1], AF.Ln,
                     sem="s_sa", n=1, label=f"p3_ln_{i}")
                e.wait("s_sa", f"p3_ln_{i}")
                e.op("activation", st3["rstd"][:, p:p + 1], st3["lnv"][:, p:p + 1], AF.Exp,
                     scale=-0.5, sem="s_sa", n=1, label=f"p3_rstd_{i}")
                e.wait("s_sa", f"p3_rstd_{i}")
                e.wait("s_ve", f"p3_nb_{i}")
                if i >= 2:
                    e.wait("s_pe", f"p3_tr_{i-2}_7")
                with _lp(nc, e.eng is not None):
                    e.op("activation", h2s[p][:, :], x1t[p][:, :], AF.Identity,
                         bias=st3["nb"][:, p:p + 1], scale=st3["rstd"][:, p:p + 1],
                         sem="s_sa", n=1, label=f"p3_norm_{i}")
                e.wait("s_ve", f"p3_te_{i}_7")
                dst = h2sp.rearrange("(dc p) t -> p dc t", p=128)[:, :, i * 128:(i + 1) * 128]
                src = h2Ts[i % 2][:, :].rearrange("p (dc t) -> p dc t", t=128)
                e.op("dma_start", dst, src, sem="d_so", n=16, label=f"p3_h2d_{i}")
            for t in range(NTILE):
                e.wait("s_ve", f"p3_gb_{t}")
                if t >= 1:
                    e.wait("s_ve", f"p3_gf_{t-1}")
                e.op("activation", gtmp["pe_"][:, :], gtmp["ls"][:, :], AF.Exp,
                     sem="s_sa", n=1, label=f"p3_ge_{t}")
            e.wait("d_so", f"p3_h2d_{NTILE-1}")
            e.wait("d_so", f"p3_x1d_{NTILE-1}")

        run_phase({"gpsimd": p3_gpsimd, "sync": p3_sync, "tensor": p3_tensor,
                   "vector": p3_vector, "scalar": p3_scalar})

        if "g" in TAPS:
            def p3_tap(e):
                e.wait("s_ve", f"p3_g_{NTILE-1}")
                e.op("dma_start", tap_g[:, :], g_sb[:, :], sem="d_x", n=16)
            run_phase({"sync": p3_tap})

        barrier()
        p3.close()

    es_r.close()

    if 4 in PHASES:
        p4 = ExitStack()
        h2q = p4.enter_context(nc.sbuf_tensor("h2q", [128, 8 * TQ], BF16))
        acc = p4.enter_context(nc.sbuf_tensor("acc", [128, 8 * TQ], F32))
        hid = p4.enter_context(nc.sbuf_tensor("hid", [128, 32 * TQ], BF16))
        w2s = p4.enter_context(nc.sbuf_tensor("w2s", [128, 32 * D], BF16))
        w1r = [p4.enter_context(nc.sbuf_tensor(f"w1r{i}", [128, 8 * 256], BF16)) for i in range(2)]
        x1t4 = [p4.enter_context(nc.sbuf_tensor(f"x1t4{i}", [128, D], F32)) for i in range(2)]
        ph4 = [p4.enter_context(nc.psum_tensor(f"ph4{i}", [128, 1024], F32)) for i in range(2)]
        po4 = [p4.enter_context(nc.psum_tensor(f"po4{i}", [128, 1024], F32)) for i in range(2)]

        NCH = 16

        def p4_sync(e):
            e.wait("d_w", "pc_w_255")
            for q in range(NQ):
                if q >= 1:
                    e.wait("s_pe", f"p4_mm1f_{q*8-1}_31")
                src = h2sp.rearrange("(dc p) t -> p dc t", p=128)[:, :, q * TQ:(q + 1) * TQ]
                dst = h2q[:, :].rearrange("p (dc t) -> p dc t", t=TQ)
                e.op("dma_start", dst, src, sem="d_x", n=16, label=f"p4_h2_{q}")
                for ee in range(E):
                    ge = q * 8 + ee
                    for c in range(NCH):
                        cg = ge * NCH + c
                        if cg >= 2:
                            pge, pc = (cg - 2) // NCH, (cg - 2) % NCH
                            e.wait("s_pe", f"p4_mm1f_{pge}_{2*pc+1}")
                        src = bass.AP(w1b.tensor, ee * D * FF + c * 256,
                                      [[FF, 128], [128 * FF, 8], [1, 256]])
                        dst = w1r[cg % 2][:, :].rearrange("p (dc f) -> p dc f", f=256)
                        e.op("dma_start", dst, src, sem="d_x", n=16, label=f"p4_w1_{cg}")
                    if ge >= 1:
                        e.wait("s_pe", f"p4_mm2t_{ge-1}_7")
                    src = bass.AP(w2b.tensor, ee * FF * D,
                                  [[D, 128], [128 * D, 32], [1, D]])
                    dst = w2s[:, :].rearrange("p (fb d) -> p fb d", d=D)
                    e.op("dma_start", dst, src, sem="d_x", n=16, label=f"p4_w2_{ge}")
                for tt in range(8):
                    gt = q * 8 + tt
                    if gt >= 2:
                        e.wait("d_so", f"p4_yd_{(gt-2)//8}_{(gt-2)%8}")
                    e.op("dma_start", x1t4[tt % 2][:, :],
                         x1sp[gt * 128:(gt + 1) * 128, :], sem="d_x", n=16,
                         label=f"p4_x1_{q}_{tt}")

        def p4_tensor(e):
            for q in range(NQ):
                for ee in range(E):
                    ge = q * 8 + ee
                    e.wait("d_x", f"p4_h2_{q}")
                    for c in range(NCH):
                        cg = ge * NCH + c
                        e.wait("d_x", f"p4_w1_{cg}")
                        for fbl in range(2):
                            fb = 2 * c + fbl
                            if fb >= 2:
                                e.wait("s_sa", f"p4_relu_{ge}_{fb-2}")
                            elif ge >= 1:
                                e.wait("s_sa", f"p4_relu_{ge-1}_{fb+30}")
                            for dc in range(8):
                                for th in range(2):
                                    e.op("matmul", ph4[fb % 2][:, th * 512:(th + 1) * 512],
                                         w1r[cg % 2][:, dc * 256 + fbl * 128: dc * 256 + fbl * 128 + 128],
                                         h2q[:, dc * TQ + th * 512: dc * TQ + (th + 1) * 512],
                                         start=(dc == 0), stop=(dc == 7),
                                         sem=("s_pe" if (dc == 7 and th == 1) else None), n=1,
                                         label=(f"p4_mm1f_{ge}_{fb}" if (dc == 7 and th == 1) else None))
                    e.wait("d_x", f"p4_w2_{ge}")
                    e.wait("s_sa", f"p4_relu_{ge}_31")
                    for tt in range(8):
                        if tt >= 2:
                            e.wait("s_ve", f"p4_acc_{ge}_{tt-2}")
                        elif ge % 8 == 0 and ge >= 1:
                            e.wait("s_ve", f"p4_cb_{ge//8-1}_{tt+6}")
                        elif ge >= 1:
                            e.wait("s_ve", f"p4_acc_{ge-1}_{tt+6}")
                        for fb in range(32):
                            for dh in range(2):
                                e.op("matmul", po4[tt % 2][:, dh * 512:(dh + 1) * 512],
                                     hid[:, fb * TQ + tt * 128: fb * TQ + (tt + 1) * 128],
                                     w2s[:, fb * D + dh * 512: fb * D + (dh + 1) * 512],
                                     start=(fb == 0), stop=(fb == 31),
                                     sem=("s_pe" if (fb == 31 and dh == 1) else None), n=1,
                                     label=(f"p4_mm2t_{ge}_{tt}" if (fb == 31 and dh == 1) else None))
                for tt in range(8):
                    ge = q * 8 + 7
                    if tt >= 2:
                        e.wait("s_ve", f"p4_cb_{q}_{tt-2}")
                    else:
                        e.wait("s_ve", f"p4_acc_{ge}_{tt+6}")
                    e.wait("s_ve", f"p4_acc_{ge}_{tt}")
                    for dh in range(2):
                        e.op("matmul", po4[tt % 2][:, dh * 512:(dh + 1) * 512],
                             GT[0:8, (q * 8 + tt) * 128:(q * 8 + tt + 1) * 128],
                             b2s[0:8, dh * 512:(dh + 1) * 512],
                             start=True, stop=True,
                             sem=("s_pe" if dh == 1 else None), n=1,
                             label=(f"p4_fc_{q}_{tt}" if dh == 1 else None))

        def p4_scalar(e):
            for q in range(NQ):
                for ee in range(E):
                    ge = q * 8 + ee
                    for fb in range(32):
                        e.wait("s_pe", f"p4_mm1f_{ge}_{fb}")
                        with _lp(nc, e.eng is not None):
                            e.op("activation", hid[:, fb * TQ:(fb + 1) * TQ], ph4[fb % 2][:, :],
                                 AF.Relu, bias=b1T[:, ee * 32 + fb: ee * 32 + fb + 1],
                                 sem="s_sa", n=1, label=f"p4_relu_{ge}_{fb}")
                    for tt in range(8):
                        e.wait("s_pe", f"p4_mm2t_{ge}_{tt}")
                        e.op("activation", po4[tt % 2][:, :], po4[tt % 2][:, :], AF.Copy,
                             scale=g_sb[:, (q * 8 + tt) * 8 + ee: (q * 8 + tt) * 8 + ee + 1],
                             sem="s_sa", n=1, label=f"p4_sc_{ge}_{tt}")
                for tt in range(8):
                    e.wait("s_ve", f"p4_y_{q}_{tt}")
                    e.op("dma_start", y_d[(q * 8 + tt) * 128:(q * 8 + tt + 1) * 128, :],
                         x1t4[tt % 2][:, :], sem="d_so", n=16, label=f"p4_yd_{q}_{tt}")
            e.wait("d_so", f"p4_yd_{NQ-1}_7")

        def p4_vector(e):
            for q in range(NQ):
                e.op("memset", acc[:, :], 0.0, sem="s_ve", n=1, label=f"p4_ms_{q}")
                for ee in range(E):
                    ge = q * 8 + ee
                    for tt in range(8):
                        e.wait("s_sa", f"p4_sc_{ge}_{tt}")
                        if ee == 0:
                            e.wait("s_ve", f"p4_ms_{q}")
                        e.op("tensor_tensor", acc[:, tt * D:(tt + 1) * D],
                             po4[tt % 2][:, :], acc[:, tt * D:(tt + 1) * D], ALU.add,
                             sem="s_ve", n=1, label=f"p4_acc_{ge}_{tt}")
                for tt in range(8):
                    e.wait("s_pe", f"p4_fc_{q}_{tt}")
                    e.wait("d_x", f"p4_x1_{q}_{tt}")
                    e.op("tensor_tensor", x1t4[tt % 2][:, :], po4[tt % 2][:, :],
                         x1t4[tt % 2][:, :], ALU.add, sem="s_ve", n=1, label=f"p4_cb_{q}_{tt}")
                    e.wait("s_ve", f"p4_cb_{q}_{tt}")
                    e.op("tensor_tensor", x1t4[tt % 2][:, :], x1t4[tt % 2][:, :],
                         acc[:, tt * D:(tt + 1) * D], ALU.add,
                         sem="s_ve", n=1, label=f"p4_y_{q}_{tt}")

        run_phase({"sync": p4_sync, "tensor": p4_tensor,
                   "scalar": p4_scalar, "vector": p4_vector})
        p4.close()

    es.close()
    return nc


_NC_CACHE = {}


def _get_nc():
    key = (tuple(sorted(TAPS)), tuple(PHASES))
    if key not in _NC_CACHE:
        _NC_CACHE[key] = build_nc()
    return _NC_CACHE[key]


def _consts():
    bf = ml_dtypes.bfloat16
    idb = np.eye(128, dtype=np.float32).astype(bf)
    idf = np.eye(128, dtype=np.float32)
    p = np.arange(128)[:, None]
    c = np.arange(896)[None, :]
    maskc = np.where(p <= c - 384, 0.0, NEG).astype(np.float32)
    onesb = np.ones((1, 128), dtype=np.float32).astype(bf)
    return {"idb": idb, "idf": idf, "maskc": maskc, "onesb": onesb}


def kernel(L0_attn_x, L0_attn_Wq, L0_attn_Wk, L0_attn_Wv, L0_attn_Wo, L0_attn_bo,
           L0_ffn_gate_w, L0_ffn_w1, L0_ffn_b1, L0_ffn_w2, L0_ffn_b2):
    nc = _get_nc()
    consts = _consts()
    base = {
        "wq": np.ascontiguousarray(L0_attn_Wq, dtype=np.float32),
        "wk": np.ascontiguousarray(L0_attn_Wk, dtype=np.float32),
        "wv": np.ascontiguousarray(L0_attn_Wv, dtype=np.float32),
        "wo": np.ascontiguousarray(L0_attn_Wo, dtype=np.float32),
        "bo": np.ascontiguousarray(L0_attn_bo, dtype=np.float32).reshape(1, D),
        "gw": np.ascontiguousarray(L0_ffn_gate_w, dtype=np.float32),
        "w1": np.ascontiguousarray(L0_ffn_w1, dtype=np.float32),
        "b1": np.ascontiguousarray(L0_ffn_b1, dtype=np.float32),
        "w2": np.ascontiguousarray(L0_ffn_w2, dtype=np.float32),
        "b2": np.ascontiguousarray(L0_ffn_b2, dtype=np.float32),
    }
    base.update(consts)
    xs = np.asarray(L0_attn_x, dtype=np.float32)
    in_maps = []
    for b in range(8):
        m = dict(base)
        m["x"] = np.ascontiguousarray(xs[b])
        in_maps.append(m)
    res = run_bass_kernel_spmd(nc, in_maps, core_ids=list(range(8)))
    out = np.stack([res.results[b]["y"] for b in range(8)], axis=0)
    return out.astype(np.float32)


def run_debug(inputs, taps, phases=(1, 2, 3, 4), cores=(0,), p2_heads=None, p2_qc=None,
              p2_dbg=None):
    global TAPS, PHASES, P2_HEADS, P2_QC, P2_DBG
    if p2_dbg is not None:
        P2_DBG = p2_dbg
    TAPS = set(taps)
    PHASES = tuple(phases)
    if p2_heads is not None:
        P2_HEADS = p2_heads
    if p2_qc is not None:
        P2_QC = p2_qc
    _NC_CACHE.clear()
    nc = _get_nc()
    consts = _consts()
    base = {
        "wq": np.ascontiguousarray(inputs["L0_attn_Wq"], dtype=np.float32),
        "wk": np.ascontiguousarray(inputs["L0_attn_Wk"], dtype=np.float32),
        "wv": np.ascontiguousarray(inputs["L0_attn_Wv"], dtype=np.float32),
        "wo": np.ascontiguousarray(inputs["L0_attn_Wo"], dtype=np.float32),
        "bo": np.ascontiguousarray(inputs["L0_attn_bo"], dtype=np.float32).reshape(1, D),
        "gw": np.ascontiguousarray(inputs["L0_ffn_gate_w"], dtype=np.float32),
        "w1": np.ascontiguousarray(inputs["L0_ffn_w1"], dtype=np.float32),
        "b1": np.ascontiguousarray(inputs["L0_ffn_b1"], dtype=np.float32),
        "w2": np.ascontiguousarray(inputs["L0_ffn_w2"], dtype=np.float32),
        "b2": np.ascontiguousarray(inputs["L0_ffn_b2"], dtype=np.float32),
    }
    base.update(consts)
    xs = np.asarray(inputs["L0_attn_x"], dtype=np.float32)
    in_maps = []
    for b in cores:
        m = dict(base)
        m["x"] = np.ascontiguousarray(xs[b])
        in_maps.append(m)
    res = run_bass_kernel_spmd(nc, in_maps, core_ids=list(cores))
    TAPS = set()
    PHASES = (1, 2, 3, 4)
    P2_HEADS = H
    P2_QC = 8
    _NC_CACHE.clear()
    return res.results


# version 08
# speedup vs baseline: 1.0147x; 1.0147x over previous
import numpy as np
import ml_dtypes
from contextlib import ExitStack

import concourse.bass as bass
import concourse.mybir as mybir
from concourse.bass_utils import run_bass_kernel_spmd

F32 = mybir.dt.float32
BF16 = mybir.dt.bfloat16
AF = mybir.ActivationFunctionType
ALU = mybir.AluOpType
AX = mybir.AxisListType

T = 4096
D = 1024
NTILE = 32
H = 16
HD = 64
E = 8
FF = 4096
NQ = 4
TQ = 1024
QC = 512
C_SHIFT = -32.0
EPS = 1e-5
QSCALE = 0.125
NEG = -1.0e30

TAPS = set()
PHASES = (1, 2, 3, 4)
P2_HEADS = H
P2_QC = 8
P2_DBG = 0

ENGS = ("tensor", "vector", "scalar", "gpsimd", "sync")


class Sched:

    def __init__(self):
        self.marks = {}
        self._c1 = {}
        self._c2 = {}
        self.counts = self._c1
        self.pass2 = False
        self.missing = set()

    def begin_pass(self, pass2):
        self.pass2 = pass2
        self.counts = self._c2 if pass2 else self._c1

    def inc(self, sem, n, label=None):
        c = self.counts.get(sem, 0) + n
        self.counts[sem] = c
        if label is not None:
            if self.pass2:
                assert self.marks.get(label) == c, (label, self.marks.get(label), c)
            else:
                assert label not in self.marks, f"dup label {label}"
                self.marks[label] = c
        return c

    def at(self, label):
        if label in self.marks:
            return self.marks[label]
        assert not self.pass2, f"unknown label {label}"
        self.missing.add(label)
        return 0


class Eng:

    def __init__(self, S, sems, eng=None):
        self.S = S
        self.sems = sems
        self.eng = eng

    def wait(self, sem, label):
        v = self.S.at(label)
        if self.eng is not None and v > 0:
            self.eng.wait_ge(self.sems[sem], v)

    def wait_val(self, sem, v):
        if self.eng is not None and v > 0:
            self.eng.wait_ge(self.sems[sem], v)

    def op(self, fname, *a, sem=None, n=1, label=None, **k):
        if self.eng is not None:
            inst = getattr(self.eng, fname)(*a, **k)
            if sem is not None:
                inst.then_inc(self.sems[sem], n)
        if sem is not None:
            self.S.inc(sem, n, label)


def _lp(nc, real):
    if real:
        return nc.allow_low_precision(reason="bf16 matmul operand rounding")
    import contextlib
    return contextlib.nullcontext()


def build_nc():
    nc = bass.Bass("TRN2", target_bir_lowering=False)

    x_d = nc.dram_tensor("x", [T, D], F32, kind="ExternalInput").ap()
    wq_d = nc.dram_tensor("wq", [D, D], F32, kind="ExternalInput").ap()
    wk_d = nc.dram_tensor("wk", [D, D], F32, kind="ExternalInput").ap()
    wv_d = nc.dram_tensor("wv", [D, D], F32, kind="ExternalInput").ap()
    wo_d = nc.dram_tensor("wo", [D, D], F32, kind="ExternalInput").ap()
    bo_d = nc.dram_tensor("bo", [1, D], F32, kind="ExternalInput").ap()
    gw_d = nc.dram_tensor("gw", [E, D], F32, kind="ExternalInput").ap()
    w1_d = nc.dram_tensor("w1", [E, D, FF], F32, kind="ExternalInput").ap()
    b1_d = nc.dram_tensor("b1", [E, FF], F32, kind="ExternalInput").ap()
    w2_d = nc.dram_tensor("w2", [E, FF, D], F32, kind="ExternalInput").ap()
    b2_d = nc.dram_tensor("b2", [E, D], F32, kind="ExternalInput").ap()
    idb_d = nc.dram_tensor("idb", [128, 128], BF16, kind="ExternalInput").ap()
    idf_d = nc.dram_tensor("idf", [128, 128], F32, kind="ExternalInput").ap()
    mask_d = nc.dram_tensor("maskc", [128, 896], F32, kind="ExternalInput").ap()
    ones_d = nc.dram_tensor("onesb", [1, 128], BF16, kind="ExternalInput").ap()

    y_d = nc.dram_tensor("y", [T, D], F32, kind="ExternalOutput").ap()

    def scratch(name, shape, dt, tap):
        kind = "ExternalOutput" if tap in TAPS else "Internal"
        return nc.dram_tensor(name, shape, dt, kind=kind).ap()

    _wk = "ExternalOutput" if "wimg" in TAPS else "Internal"
    w1b = nc.dram_tensor("w1b", [E, D, FF], BF16, kind=_wk).ap()
    w2b = nc.dram_tensor("w2b", [E, FF, D], BF16, kind=_wk).ap()
    qsp = scratch("qsp", [D, T], BF16, "qsp")
    ksp = scratch("ksp", [D, T], BF16, "ksp")
    x1sp = scratch("x1sp", [T, D], F32, "x1sp")
    h2sp = scratch("h2sp", [D, T], BF16, "h2sp")
    tap_h1T = scratch("tap_h1T", [128, 8 * T], BF16, "h1T") if "h1T" in TAPS else None
    tap_vaug = scratch("tap_vaug", [128, H * 32 * 65], BF16, "vaug") if "vaug" in TAPS else None
    tap_oT = scratch("tap_oT", [128, 8 * T], BF16, "oT") if "oT" in TAPS else None
    tap_g = scratch("tap_g", [128, NTILE * E], F32, "g") if "g" in TAPS else None

    S = Sched()
    es = ExitStack()

    sems = {}
    for s in ("d_x", "d_so", "d_w", "s_pe", "s_sa", "s_ve", "s_gp", "s_bar"):
        sems[s] = es.enter_context(nc.semaphore(s))

    idb = es.enter_context(nc.sbuf_tensor("idb_s", [128, 128], BF16))
    idf = es.enter_context(nc.sbuf_tensor("idf_s", [128, 128], F32))
    onesb = es.enter_context(nc.sbuf_tensor("ones_s", [1, 128], BF16))
    g_sb = es.enter_context(nc.sbuf_tensor("g_sb", [128, NTILE * E], F32))
    GT = es.enter_context(nc.sbuf_tensor("GT", [8, T], BF16))
    b2s = es.enter_context(nc.sbuf_tensor("b2s", [8, D], BF16))
    b1T = es.enter_context(nc.sbuf_tensor("b1T", [128, E * 32], F32))
    cm32 = es.enter_context(nc.sbuf_tensor("cm32", [128, 1], F32))

    es_r = ExitStack()
    vaug = es_r.enter_context(nc.sbuf_tensor("vaug", [128, H * 32 * 65], BF16, side="right"))

    barrier_n = [0]

    def run_phase(fns):
        S.begin_pass(False)
        for en in ENGS:
            if en in fns:
                fns[en](Eng(S, sems))
        missing = S.missing - set(S.marks)
        assert not missing, f"labels never defined: {sorted(missing)[:10]}"
        S.missing = set()
        S.begin_pass(True)
        with nc.Block() as blk:
            for en in ENGS:
                if en in fns:
                    getattr(blk, en)(lambda eng, fn=fns[en]: fn(Eng(S, sems, eng)))

    def barrier():
        barrier_n[0] += 1
        tgt = 5 * barrier_n[0]

        def mk(en):
            def f(e):
                e.op("nop", sem="s_bar", n=1)
                e.wait_val("s_bar", tgt)
            return f
        run_phase({en: mk(en) for en in ENGS})

    def p0_gpsimd(e):
        e.op("memset", cm32[:, :], C_SHIFT, sem="s_gp", n=1, label="c_m32")

    def p0_sync(e):
        e.op("dma_start", idb[:, :], idb_d[:, :], sem="d_x", n=16, label="c_idb")
        e.op("dma_start", idf[:, :], idf_d[:, :], sem="d_x", n=16, label="c_idf")
        e.op("dma_start", onesb[:, :], ones_d[:, :], sem="d_x", n=16, label="c_ones")

    run_phase({"sync": p0_sync, "gpsimd": p0_gpsimd})

    if 1 in PHASES:
        p1 = ExitStack()
        xt = [p1.enter_context(nc.sbuf_tensor(f"xt{i}", [128, D], F32)) for i in range(3)]
        h1s = [p1.enter_context(nc.sbuf_tensor(f"h1s{i}", [128, D], BF16)) for i in range(2)]
        sqd = p1.enter_context(nc.sbuf_tensor("sqd", [128, D], BF16))
        st = {}
        for nm in ("ssum", "ssq", "mm_", "vv", "m2v", "var0", "lnv", "rstd", "nb"):
            st[nm] = p1.enter_context(nc.sbuf_tensor(nm, [128, 2], F32))
        h1T = p1.enter_context(nc.sbuf_tensor("h1T", [128, 8 * T], BF16))
        wbuf = [p1.enter_context(nc.sbuf_tensor(f"wbuf{i}", [128, 8 * D], BF16))
                for i in range(2)]
        wqs, wks, wvs = wbuf[0], wbuf[1], wbuf[0]
        qks = [p1.enter_context(nc.sbuf_tensor(f"qks{i}", [128, 512], BF16)) for i in range(4)]
        pt = [p1.enter_context(nc.psum_tensor(f"pt{i}", [128, 1024], BF16)) for i in range(2)]
        psq = [p1.enter_context(nc.psum_tensor(f"psq{i}", [128, 512], F32)) for i in range(2)]
        psv = [p1.enter_context(nc.psum_tensor(f"psv{i}", [128, 512], F32)) for i in range(2)]

        def p1_gpsimd(e):
            for nm, sb, dr in (("q", wqs, wq_d), ("k", wks, wk_d)):
                src = dr.rearrange("(dc p) c -> p dc c", p=128)
                dst = sb[:, :].rearrange("p (dc c) -> p dc c", c=D)
                e.op("dma_start", dst, src, sem="d_w", n=16, label=f"p1_w{nm}")
            e.wait("s_pe", "p1_qk_0_7_7")
            src = wv_d.rearrange("(dc p) c -> p dc c", p=128)
            dst = wvs[:, :].rearrange("p (dc c) -> p dc c", c=D)
            e.op("dma_start", dst, src, sem="d_w", n=16, label="p1_wv")
            va = bass.AP(vaug, HD, [[vaug.shape[1], 128], [65, H * 32], [1, 1]])
            e.op("memset", va, 1.0, sem="s_gp", n=1, label="p1_vones")

        def p1_sync(e):
            for i in range(NTILE):
                if i >= 3:
                    e.wait("s_sa", f"p1_norm_{i-3}")
                e.op("dma_start", xt[i % 3][:, :], x_d[i * 128:(i + 1) * 128, :],
                     sem="d_x", n=16, label=f"p1_x_{i}")

        def p1_vector(e):
            inv_d = 1.0 / D
            for i in range(NTILE):
                p = i % 2
                e.wait("d_x", f"p1_x_{i}")
                e.op("tensor_reduce", st["ssum"][:, p:p + 1], xt[i % 3][:, :], AX.X, ALU.add,
                     sem="s_ve", n=1, label=f"p1_sum_{i}")
                e.wait("s_ve", f"p1_sum_{i}")
                e.op("tensor_scalar", st["mm_"][:, p:p + 1], st["ssum"][:, p:p + 1],
                     inv_d, 0.0, ALU.mult, ALU.bypass, sem="s_ve", n=1, label=f"p1_m_{i}")
                e.wait("s_ve", f"p1_m_{i}")
                e.wait("s_sa", f"p1_sq_{i}")
                e.op("tensor_scalar", st["vv"][:, p:p + 1], st["ssq"][:, p:p + 1],
                     inv_d, EPS, ALU.mult, ALU.add, sem="s_ve", n=1, label=f"p1_vv_{i}")
                e.op("tensor_tensor", st["m2v"][:, p:p + 1], st["mm_"][:, p:p + 1],
                     st["mm_"][:, p:p + 1], ALU.mult, sem="s_ve", n=1, label=f"p1_m2_{i}")
                e.wait("s_ve", f"p1_m2_{i}")
                e.op("tensor_tensor", st["var0"][:, p:p + 1], st["vv"][:, p:p + 1],
                     st["m2v"][:, p:p + 1], ALU.subtract, sem="s_ve", n=1, label=f"p1_var_{i}")
                e.wait("s_sa", f"p1_rstd_{i}")
                e.op("tensor_scalar", st["nb"][:, p:p + 1], st["mm_"][:, p:p + 1],
                     st["rstd"][:, p:p + 1], -1.0, ALU.mult, ALU.mult,
                     sem="s_ve", n=1, label=f"p1_nb_{i}")
                if i > 0:
                    for dc in range(8):
                        j = (i - 1) * 8 + dc
                        e.wait("s_pe", f"p1_tr_{i-1}_{dc}")
                        e.op("tensor_copy", h1T[:, dc * T + (i - 1) * 128: dc * T + i * 128],
                             pt[j % 2][:, 0:128], sem="s_ve", n=1, label=f"p1_te_{i-1}_{dc}")
            for dc in range(8):
                j = (NTILE - 1) * 8 + dc
                e.wait("s_pe", f"p1_tr_{NTILE-1}_{dc}")
                e.op("tensor_copy", h1T[:, dc * T + (NTILE - 1) * 128: dc * T + NTILE * 128],
                     pt[j % 2][:, 0:128], sem="s_ve", n=1, label=f"p1_te_{NTILE-1}_{dc}")
            for tt in range(NTILE):
                for dh in range(2):
                    j = tt * 2 + dh
                    e.wait("s_pe", f"p1_v_{tt}_{dh}")
                    dst = bass.AP(vaug, ((dh * 8) * 32 + tt) * 65,
                                  [[vaug.shape[1], 128], [32 * 65, 8], [1, 64]])
                    src = psv[j % 2][:, :].rearrange("p (h c) -> p h c", c=64)
                    with _lp(nc, e.eng is not None):
                        e.op("tensor_copy", dst, src, sem="s_ve", n=1, label=f"p1_ve_{tt}_{dh}")

        def p1_scalar(e):
            for i in range(NTILE):
                p = i % 2
                e.wait("d_x", f"p1_x_{i}")
                if i >= 2:
                    e.wait("s_sa", f"p1_sq_{i-2}")
                    e.wait("s_ve", f"p1_vv_{i-2}")
                with _lp(nc, e.eng is not None):
                    e.op("activation", sqd[:, :], xt[i % 3][:, :], AF.Square,
                         accum_out=st["ssq"][:, p:p + 1], sem="s_sa", n=1, label=f"p1_sq_{i}")
                e.wait("s_ve", f"p1_var_{i}")
                e.op("activation", st["lnv"][:, p:p + 1], st["var0"][:, p:p + 1], AF.Ln,
                     sem="s_sa", n=1, label=f"p1_ln_{i}")
                e.wait("s_sa", f"p1_ln_{i}")
                e.op("activation", st["rstd"][:, p:p + 1], st["lnv"][:, p:p + 1], AF.Exp,
                     scale=-0.5, sem="s_sa", n=1, label=f"p1_rstd_{i}")
                e.wait("s_sa", f"p1_rstd_{i}")
                e.wait("s_ve", f"p1_nb_{i}")
                if i >= 2:
                    e.wait("s_pe", f"p1_tr_{i-2}_7")
                with _lp(nc, e.eng is not None):
                    e.op("activation", h1s[p][:, :], xt[i % 3][:, :], AF.Identity,
                         bias=st["nb"][:, p:p + 1], scale=st["rstd"][:, p:p + 1],
                         sem="s_sa", n=1, label=f"p1_norm_{i}")
            for kk, (sc, spd) in enumerate(((QSCALE, qsp), (1.0, ksp))):
                for dt in range(8):
                    for tc in range(8):
                        j = (kk * 8 + dt) * 8 + tc
                        e.wait("s_pe", f"p1_qk_{kk}_{dt}_{tc}")
                        if j >= 4:
                            e.wait("d_so", f"p1_qkd_{j-4}")
                        with _lp(nc, e.eng is not None):
                            e.op("activation", qks[j % 4][:, :], psq[j % 2][:, :], AF.Copy,
                                 scale=sc, sem="s_sa", n=1, label=f"p1_qke_{j}")
                        e.wait("s_sa", f"p1_qke_{j}")
                        e.op("dma_start", spd[dt * 128:(dt + 1) * 128, tc * 512:(tc + 1) * 512],
                             qks[j % 4][:, :], sem="d_so", n=16, label=f"p1_qkd_{j}")
            e.wait("d_so", f"p1_qkd_{127}")

        def p1_tensor(e):
            e.wait("d_x", "c_idb")
            for i in range(NTILE):
                e.wait("s_sa", f"p1_norm_{i}")
                for dc in range(8):
                    j = i * 8 + dc
                    if j >= 2:
                        e.wait("s_ve", f"p1_te_{(j-2)//8}_{(j-2)%8}")
                    e.op("transpose", pt[j % 2][:, 0:128], h1s[i % 2][:, dc * 128:(dc + 1) * 128],
                         idb[:, :], sem="s_pe", n=1, label=f"p1_tr_{i}_{dc}")
            e.wait("d_w", "p1_wq")
            e.wait("d_w", "p1_wk")
            for kk, wsb in enumerate((wqs, wks)):
                for dt in range(8):
                    for tc in range(8):
                        j = (kk * 8 + dt) * 8 + tc
                        if j >= 2:
                            e.wait("s_sa", f"p1_qke_{j-2}")
                        for dc in range(8):
                            e.op("matmul", psq[j % 2][:, :],
                                 wsb[:, dc * D + dt * 128: dc * D + (dt + 1) * 128],
                                 h1T[:, dc * T + tc * 512: dc * T + (tc + 1) * 512],
                                 start=(dc == 0), stop=(dc == 7),
                                 sem=("s_pe" if dc == 7 else None), n=1,
                                 label=(f"p1_qk_{kk}_{dt}_{tc}" if dc == 7 else None))
            e.wait("s_gp", "p1_vones")
            e.wait("d_w", "p1_wv")
            for tt in range(NTILE):
                for dh in range(2):
                    j = tt * 2 + dh
                    if j >= 2:
                        e.wait("s_ve", f"p1_ve_{(j-2)//2}_{(j-2)%2}")
                    for dc in range(8):
                        e.op("matmul", psv[j % 2][:, :],
                             h1T[:, dc * T + tt * 128: dc * T + (tt + 1) * 128],
                             wvs[:, dc * D + dh * 512: dc * D + (dh + 1) * 512],
                             start=(dc == 0), stop=(dc == 7),
                             sem=("s_pe" if dc == 7 else None), n=1,
                             label=(f"p1_v_{tt}_{dh}" if dc == 7 else None))

        run_phase({"gpsimd": p1_gpsimd, "sync": p1_sync, "vector": p1_vector,
                   "scalar": p1_scalar, "tensor": p1_tensor})

        if "h1T" in TAPS or "vaug" in TAPS:
            def p1_tap(e):
                e.wait("s_ve", f"p1_te_{NTILE-1}_7")
                e.wait("s_ve", f"p1_ve_{NTILE-1}_1")
                last = None
                if tap_h1T is not None:
                    e.op("dma_start", tap_h1T[:, :], h1T[:, :], sem="d_x", n=16,
                         label="p1_tapd1")
                    last = "p1_tapd1"
                if tap_vaug is not None:
                    e.op("dma_start", tap_vaug[:, :], vaug[:, :], sem="d_x", n=16,
                         label="p1_tapd2")
                    last = "p1_tapd2"
                e.wait("d_x", last)
            run_phase({"sync": p1_tap})

        barrier()
        p1.close()

    oT = es_r.enter_context(nc.sbuf_tensor("oT", [128, 8 * T], BF16, side="right"))
    if 2 in PHASES:
        p2 = ExitStack()
        qh = [p2.enter_context(nc.sbuf_tensor(f"qh{i}", [64, T], BF16)) for i in range(2)]
        kh = [p2.enter_context(nc.sbuf_tensor(f"kh{i}", [64, T], BF16)) for i in range(2)]
        pbuf = [p2.enter_context(nc.sbuf_tensor(f"pbuf{i}", [128, 512], BF16)) for i in range(4)]
        stmp = [p2.enter_context(nc.sbuf_tensor(f"stmp{i}", [128, 512], F32)) for i in range(2)]
        masks = p2.enter_context(nc.sbuf_tensor("masks", [128, 896], F32))
        linvf = p2.enter_context(nc.sbuf_tensor("linvf", [1, 512], F32))
        linvb = p2.enter_context(nc.sbuf_tensor("linvb", [1, 512], BF16))
        linvs = [p2.enter_context(nc.sbuf_tensor(f"linvs{i}", [64, 512], F32)) for i in range(2)]
        pcs = [p2.enter_context(nc.sbuf_tensor(f"pcs{i}", [128, 2048], BF16)) for i in range(2)]
        ps_s = [p2.enter_context(nc.psum_tensor(f"ps_s{i}", [128, 512], F32)) for i in range(4)]
        ps_o = [p2.enter_context(nc.psum_tensor(f"ps_o{i}", [65, 512], F32)) for i in range(2)]
        ps_l = [p2.enter_context(nc.psum_tensor(f"ps_l{i}", [64, 512], F32)) for i in range(2)]

        def p2_iter():
            gs = 0
            gm = 0
            for h in range(P2_HEADS):
                for qc in range(P2_QC):
                    nstrip = 4 * (qc + 1)
                    strips = []
                    for kt in range(nstrip):
                        masked = kt >= nstrip - 4
                        strips.append((kt, gs, gm if masked else None, masked))
                        gs += 1
                        if masked:
                            gm += 1
                    yield h, qc, nstrip, strips

        P2IT = list(p2_iter())
        GS_OF_M = {}
        for h, qc, nstrip, strips in P2IT:
            for kt, gs, gm, masked in strips:
                if masked:
                    GS_OF_M[gm] = gs

        def p2_gpsimd(e):
            j = 0
            for ee in range(E):
                for blk in range(16):
                    if j >= 2:
                        e.wait("d_w", f"pc_w_{j-2}")
                    r0, fh = (blk // 2) * 128, (blk % 2) * 2048
                    e.op("dma_start", pcs[j % 2][:, :],
                         w1_d[ee, r0:r0 + 128, fh:fh + 2048],
                         max_dma_last_dim=8192, sem="d_w", n=16, label=f"pc_r_{j}")
                    e.wait("d_w", f"pc_r_{j}")
                    e.op("dma_start", w1b[ee, r0:r0 + 128, fh:fh + 2048],
                         pcs[j % 2][:, :], sem="d_w", n=16, label=f"pc_w_{j}")
                    j += 1
                for blk in range(16):
                    if j >= 2:
                        e.wait("d_w", f"pc_w_{j-2}")
                    src = bass.AP(w2_d.tensor, ee * FF * D + blk * 256 * D,
                                  [[D, 128], [128 * D, 2], [1, D]])
                    e.op("dma_start", pcs[j % 2][:, :].rearrange("p (r c) -> p r c", c=D),
                         src, max_dma_last_dim=8192, sem="d_w", n=16, label=f"pc_r_{j}")
                    e.wait("d_w", f"pc_r_{j}")
                    dst = bass.AP(w2b.tensor, ee * FF * D + blk * 256 * D,
                                  [[D, 128], [128 * D, 2], [1, D]])
                    e.op("dma_start", dst, pcs[j % 2][:, :].rearrange("p (r c) -> p r c", c=D),
                         sem="d_w", n=16, label=f"pc_w_{j}")
                    j += 1

        def p2_sync(e):
            e.op("dma_start", masks[:, :], mask_d[:, :], sem="d_x", n=16, label="p2_mask")
            last_gp_of_head = {}
            for hh, qq, npp, prs in P2IT:
                last_gp_of_head[hh] = prs[-1][1]
            for h in range(P2_HEADS):
                if h >= 2:
                    e.wait("s_pe", f"p2_pv_{last_gp_of_head[h-2]}")
                e.op("dma_start", qh[h % 2][:, :], qsp[h * 64:(h + 1) * 64, :],
                     sem="d_x", n=16, label=f"p2_q_{h}")
                e.op("dma_start", kh[h % 2][:, :], ksp[h * 64:(h + 1) * 64, :],
                     sem="d_x", n=16, label=f"p2_k_{h}")

        def p2_tensor(e):
            if 1 in PHASES:
                e.wait("s_ve", f"p1_ve_{NTILE-1}_1")
            pend = None
            for h, qc, nstrip, strips in P2IT:
                idx = h * 8 + qc
                op_ = idx % 2
                e.wait("d_x", f"p2_q_{h}")
                e.wait("d_x", f"p2_k_{h}")
                def emit_s(kt, gs):
                    e.op("matmul", ps_s[gs % 4][:, :],
                         kh[h % 2][0:64, kt * 128:(kt + 1) * 128],
                         qh[h % 2][0:64, qc * 512:(qc + 1) * 512],
                         start=True, stop=True, sem="s_pe", n=1, label=f"p2_s_{gs}")

                def emit_pv(kt, gs):
                    e.wait("s_sa", f"p2_exp_{gs}")
                    e.op("matmul", ps_o[op_][:, :],
                         vaug[:, (h * 32 + kt) * 65: (h * 32 + kt) * 65 + 65],
                         pbuf[gs % 4][:, :],
                         start=(kt == 0), stop=(kt == nstrip - 1),
                         sem="s_pe", n=1, label=f"p2_pv_{gs}")

                DEPTH = 3
                lead = min(DEPTH, nstrip)
                for j in range(lead):
                    emit_s(strips[j][0], strips[j][1])
                if pend is not None:
                    pidx = pend
                    e.wait("s_ve", f"p2_linv_{pidx}")
                    if pidx >= 2:
                        e.wait("s_sa", f"p2_lc_{pidx-2}")
                    e.op("matmul", ps_l[pidx % 2][:, :], onesb[0:1, 0:64],
                         linvb[0:1, :], start=True, stop=True,
                         sem="s_pe", n=1, label=f"p2_lb_{pidx}")
                    pend = None
                if idx >= 2:
                    e.wait("s_ve", f"p2_otev_{idx-2}")
                for j in range(lead, nstrip):
                    emit_s(strips[j][0], strips[j][1])
                    emit_pv(strips[j - lead][0], strips[j - lead][1])
                for j in range(nstrip - lead, nstrip):
                    emit_pv(strips[j][0], strips[j][1])
                pend = idx
            pidx = pend
            e.wait("s_ve", f"p2_linv_{pidx}")
            if pidx >= 2:
                e.wait("s_sa", f"p2_lc_{pidx-2}")
            e.op("matmul", ps_l[pidx % 2][:, :], onesb[0:1, 0:64], linvb[0:1, :],
                 start=True, stop=True, sem="s_pe", n=1, label=f"p2_lb_{pidx}")

        def p2_vector(e):
            e.wait("d_x", "p2_mask")
            for h, qc, nstrip, strips in P2IT:
                idx = h * 8 + qc
                op_ = idx % 2
                if idx >= 1:
                    pidx = idx - 1
                    ph_, pqc = pidx // 8, pidx % 8
                    e.wait("s_sa", f"p2_lc_{pidx}")
                    dst = oT[(ph_ % 2) * 64:(ph_ % 2) * 64 + 64,
                             (ph_ // 2) * T + pqc * 512:(ph_ // 2) * T + (pqc + 1) * 512]
                    with _lp(nc, e.eng is not None):
                        e.op("tensor_tensor", dst, ps_o[pidx % 2][0:64, :], linvs[pidx % 2][:, :],
                             ALU.mult, sem="s_ve", n=1, label=f"p2_otev_{pidx}")
                for kt, gs, gm, masked in strips:
                    if not masked:
                        continue
                    e.wait("s_pe", f"p2_s_{gs}")
                    if gm >= 2:
                        e.wait("s_sa", f"p2_exp_{GS_OF_M[gm-2]}")
                    i_rel = kt - 4 * qc
                    e.op("tensor_tensor", stmp[gm % 2][:, :],
                         ps_s[gs % 4][:, :],
                         masks[:, 384 - 128 * i_rel: 896 - 128 * i_rel],
                         ALU.add, sem="s_ve", n=1, label=f"p2_m_{gs}")
                e.wait("s_pe", f"p2_pv_{strips[-1][1]}")
                e.op("reciprocal", linvf[:, :], ps_o[op_][64:65, :], sem="s_ve", n=1,
                     label=f"p2_rc_{idx}")
                e.wait("s_ve", f"p2_rc_{idx}")
                if idx >= 1:
                    e.wait("s_pe", f"p2_lb_{idx-1}")
                with _lp(nc, e.eng is not None):
                    e.op("tensor_copy", linvb[:, :], linvf[:, :], sem="s_ve", n=1,
                         label=f"p2_linv_{idx}")
            pidx = (P2_HEADS - 1) * 8 + (P2_QC - 1)
            ph_, pqc = pidx // 8, pidx % 8
            e.wait("s_sa", f"p2_lc_{pidx}")
            dst = oT[(ph_ % 2) * 64:(ph_ % 2) * 64 + 64,
                     (ph_ // 2) * T + pqc * 512:(ph_ // 2) * T + (pqc + 1) * 512]
            with _lp(nc, e.eng is not None):
                e.op("tensor_tensor", dst, ps_o[pidx % 2][0:64, :], linvs[pidx % 2][:, :],
                     ALU.mult, sem="s_ve", n=1, label=f"p2_otev_{pidx}")

        def p2_scalar(e):
            for h, qc, nstrip, strips in P2IT:
                idx = h * 8 + qc
                if idx >= 1:
                    pidx = idx - 1
                    e.wait("s_pe", f"p2_lb_{pidx}")
                    if pidx >= 2:
                        e.wait("s_ve", f"p2_otev_{pidx-2}")
                    e.op("copy", linvs[pidx % 2][:, :], ps_l[pidx % 2][:, :],
                         sem="s_sa", n=1, label=f"p2_lc_{pidx}")
                for kt, gs, gm, masked in strips:
                    if masked:
                        e.wait("s_ve", f"p2_m_{gs}")
                        src = stmp[gm % 2][:, :]
                    else:
                        e.wait("s_pe", f"p2_s_{gs}")
                        src = ps_s[gs % 4][:, :]
                    if gs == 0:
                        e.wait("s_gp", "c_m32")
                    with _lp(nc, e.eng is not None):
                        e.op("activation", pbuf[gs % 4][:, :], src, AF.Exp, bias=cm32[:, 0:1],
                             sem="s_sa", n=1, label=f"p2_exp_{gs}")
            pidx = (P2_HEADS - 1) * 8 + (P2_QC - 1)
            e.wait("s_pe", f"p2_lb_{pidx}")
            e.op("copy", linvs[pidx % 2][:, :], ps_l[pidx % 2][:, :],
                 sem="s_sa", n=1, label=f"p2_lc_{pidx}")

        run_phase({"sync": p2_sync, "tensor": p2_tensor, "vector": p2_vector,
                   "scalar": p2_scalar, "gpsimd": p2_gpsimd})

        if "oT" in TAPS:
            def p2_tap(e):
                e.wait("s_ve", f"p2_otev_{(P2_HEADS-1)*8 + (P2_QC-1)}")
                e.op("dma_start", tap_oT[:, :], oT[:, :], sem="d_x", n=16)
            run_phase({"sync": p2_tap})

        barrier()
        p2.close()

    if 3 in PHASES:
        p3 = ExitStack()
        wos = p3.enter_context(nc.sbuf_tensor("wos", [128, 8 * D], BF16))
        bob = p3.enter_context(nc.sbuf_tensor("bob", [128, D], F32))
        xt3 = [p3.enter_context(nc.sbuf_tensor(f"xt3{i}", [128, D], F32)) for i in range(3)]
        x1t = [p3.enter_context(nc.sbuf_tensor(f"x1t{i}", [128, D], F32)) for i in range(2)]
        h2s = [p3.enter_context(nc.sbuf_tensor(f"h2s{i}", [128, D], BF16)) for i in range(2)]
        h2Ts = [p3.enter_context(nc.sbuf_tensor(f"h2Ts{i}", [128, D], BF16)) for i in range(2)]
        sqd3 = p3.enter_context(nc.sbuf_tensor("sqd3", [128, D], BF16))
        st3 = {}
        for nm in ("ssum", "ssq", "mm_", "vv", "m2v", "var0", "lnv", "rstd", "nb"):
            st3[nm] = p3.enter_context(nc.sbuf_tensor("g_" + nm, [128, 2], F32))
        gws = p3.enter_context(nc.sbuf_tensor("gws", [8, D], F32))
        gwT = p3.enter_context(nc.sbuf_tensor("gwT", [128, 64], BF16))
        lgs = p3.enter_context(nc.sbuf_tensor("lgs", [128, NTILE * E], F32))
        b1stg = p3.enter_context(nc.sbuf_tensor("b1stg", [32, 128], F32))
        gtmp = {}
        for nm in ("m1", "se", "ri", "m1p", "m2", ):
            gtmp[nm] = p3.enter_context(nc.sbuf_tensor("t_" + nm, [128, 1], F32))
        for nm in ("ls", "pe_", "pr_", "mk", "pm", "kp"):
            gtmp[nm] = p3.enter_context(nc.sbuf_tensor("t_" + nm, [128, 8], F32))
        ps_ao = [p3.enter_context(nc.psum_tensor(f"ps_ao{i}", [128, 1024], F32)) for i in range(2)]
        pt3 = [p3.enter_context(nc.psum_tensor(f"pt3{i}", [128, 1024], BF16)) for i in range(2)]
        ps_lg = [p3.enter_context(nc.psum_tensor(f"ps_lg{i}", [128, 512], F32)) for i in range(2)]

        def p3_gpsimd(e):
            src = wo_d.rearrange("(dc p) c -> p dc c", p=128)
            dst = wos[:, :].rearrange("p (dc c) -> p dc c", c=D)
            e.op("dma_start", dst, src, sem="d_w", n=16, label="p3_wo")
            e.op("dma_start", b2s[:, :], b2_d[:, :], sem="d_w", n=16, label="p3_b2")

        def p3_sync(e):
            e.op("dma_start", bob[:, :], bass.AP(bo_d.tensor, 0, [[0, 128], [1, D]]),
                 sem="d_x", n=16, label="p3_bo")
            e.op("dma_start", gws[:, :], gw_d[:, :], sem="d_x", n=16, label="p3_gw")
            b1r = b1_d.rearrange("e (r c) -> e r c", c=128)
            for ee in range(E):
                if ee >= 1:
                    e.wait("s_pe", f"p3_b1t_{ee-1}")
                e.op("dma_start", b1stg[:, :], b1r[ee], sem="d_x", n=16, label=f"p3_b1_{ee}")
            for i in range(NTILE):
                if i >= 3:
                    e.wait("s_ve", f"p3_x1_{i-3}")
                e.op("dma_start", xt3[i % 3][:, :], x_d[i * 128:(i + 1) * 128, :],
                     sem="d_x", n=16, label=f"p3_x_{i}")

        def p3_tensor(e):
            if 2 in PHASES:
                _pl = (P2_HEADS - 1) * 8 + (P2_QC - 1)
                e.wait("s_ve", f"p2_otev_{_pl}")
            e.wait("d_x", "p3_gw")
            e.wait("d_x", "c_idf")
            for dc in range(8):
                if dc >= 2:
                    e.wait("s_sa", f"p3_gwe_{dc-2}")
                e.op("transpose", ps_lg[dc % 2][0:128, 0:8], gws[0:8, dc * 128:(dc + 1) * 128],
                     idf[0:8, 0:8], sem="s_pe", n=1, label=f"p3_gwt_{dc}")
            for ee in range(E):
                e.wait("d_x", f"p3_b1_{ee}")
                j = 8 + ee
                if ee >= 2:
                    e.wait("s_ve", f"p3_b1e_{ee-2}")
                else:
                    e.wait("s_sa", f"p3_gwe_{6+ee}")
                e.op("transpose", ps_lg[j % 2][0:128, 0:32], b1stg[:, :], idf[0:32, 0:32],
                     sem="s_pe", n=1, label=f"p3_b1t_{ee}")
            for i in range(NTILE):
                e.wait("d_w", "p3_wo")
                if i >= 2:
                    e.wait("s_ve", f"p3_x1_{i-2}")
                for dc in range(8):
                    for dh in range(2):
                        e.op("matmul", ps_ao[i % 2][:, dh * 512:(dh + 1) * 512],
                             oT[:, dc * T + i * 128: dc * T + (i + 1) * 128],
                             wos[:, dc * D + dh * 512: dc * D + (dh + 1) * 512],
                             start=(dc == 0), stop=(dc == 7),
                             sem=("s_pe" if (dc == 7 and dh == 1) else None), n=1,
                             label=(f"p3_ao_{i}" if (dc == 7 and dh == 1) else None))
                e.wait("s_sa", f"p3_norm_{i}")
                for dc in range(8):
                    j = i * 8 + dc
                    if j >= 2:
                        e.wait("s_ve", f"p3_te_{(j-2)//8}_{(j-2)%8}")
                    e.op("transpose", pt3[j % 2][:, 0:128], h2s[i % 2][:, dc * 128:(dc + 1) * 128],
                         idb[:, :], sem="s_pe", n=1, label=f"p3_tr_{i}_{dc}")
                e.wait("s_ve", f"p3_te_{i}_7")
                e.wait("s_sa", "p3_gwe_7")
                if i >= 2:
                    e.wait("s_ve", f"p3_lge_{i-2}")
                else:
                    e.wait("s_ve", f"p3_b1e_{6+i}")
                for dc in range(8):
                    e.op("matmul", ps_lg[i % 2][0:128, 0:8],
                         h2Ts[i % 2][:, dc * 128:(dc + 1) * 128], gwT[:, dc * 8:(dc + 1) * 8],
                         start=(dc == 0), stop=(dc == 7),
                         sem=("s_pe" if dc == 7 else None), n=1,
                         label=(f"p3_lg_{i}" if dc == 7 else None))
            for t in range(NTILE):
                e.wait("s_ve", f"p3_g_{t}")
                j = t % 2
                if t >= 2:
                    e.wait("s_ve", f"p3_gte_{t-2}")
                else:
                    e.wait("s_ve", f"p3_lge_{30+t}")
                e.op("transpose", ps_lg[j][0:8, 0:128], g_sb[:, t * 8:(t + 1) * 8], idf[:, :],
                     sem="s_pe", n=1, label=f"p3_gt_{t}")

        def p3_vector(e):
            inv_d = 1.0 / D
            for ee in range(E):
                e.wait("s_pe", f"p3_b1t_{ee}")
                e.op("tensor_copy", b1T[:, ee * 32:(ee + 1) * 32],
                     ps_lg[(8 + ee) % 2][0:128, 0:32], sem="s_ve", n=1, label=f"p3_b1e_{ee}")
            for i in range(NTILE):
                p = i % 2
                e.wait("s_pe", f"p3_ao_{i}")
                e.wait("d_x", f"p3_x_{i}")
                e.wait("d_x", "p3_bo")
                if i >= 2:
                    e.wait("d_so", f"p3_x1d_{i-2}")
                    e.wait("s_sa", f"p3_norm_{i-2}")
                e.op("tensor_tensor", x1t[p][:, :], ps_ao[p][:, :], bob[:, :], ALU.add,
                     sem="s_ve", n=1, label=f"p3_x1a_{i}")
                e.wait("s_ve", f"p3_x1a_{i}")
                e.op("tensor_tensor", x1t[p][:, :], x1t[p][:, :], xt3[i % 3][:, :], ALU.add,
                     sem="s_ve", n=1, label=f"p3_x1_{i}")
                e.wait("s_ve", f"p3_x1_{i}")
                e.op("tensor_reduce", st3["ssum"][:, p:p + 1], x1t[p][:, :], AX.X, ALU.add,
                     sem="s_ve", n=1, label=f"p3_sum_{i}")
                e.wait("s_ve", f"p3_sum_{i}")
                e.op("tensor_scalar", st3["mm_"][:, p:p + 1], st3["ssum"][:, p:p + 1],
                     inv_d, 0.0, ALU.mult, ALU.bypass, sem="s_ve", n=1, label=f"p3_m_{i}")
                e.wait("s_ve", f"p3_m_{i}")
                e.wait("s_sa", f"p3_sq_{i}")
                e.op("tensor_scalar", st3["vv"][:, p:p + 1], st3["ssq"][:, p:p + 1],
                     inv_d, EPS, ALU.mult, ALU.add, sem="s_ve", n=1, label=f"p3_vv_{i}")
                e.op("tensor_tensor", st3["m2v"][:, p:p + 1], st3["mm_"][:, p:p + 1],
                     st3["mm_"][:, p:p + 1], ALU.mult, sem="s_ve", n=1, label=f"p3_m2_{i}")
                e.wait("s_ve", f"p3_m2_{i}")
                e.op("tensor_tensor", st3["var0"][:, p:p + 1], st3["vv"][:, p:p + 1],
                     st3["m2v"][:, p:p + 1], ALU.subtract, sem="s_ve", n=1, label=f"p3_var_{i}")
                e.wait("s_sa", f"p3_rstd_{i}")
                e.op("tensor_scalar", st3["nb"][:, p:p + 1], st3["mm_"][:, p:p + 1],
                     st3["rstd"][:, p:p + 1], -1.0, ALU.mult, ALU.mult,
                     sem="s_ve", n=1, label=f"p3_nb_{i}")
                for dc in range(8):
                    j = i * 8 + dc
                    e.wait("s_pe", f"p3_tr_{i}_{dc}")
                    if i >= 2 and dc == 0:
                        e.wait("d_so", f"p3_h2d_{i-2}")
                        e.wait("s_pe", f"p3_lg_{i-2}")
                    e.op("tensor_copy", h2Ts[i % 2][:, dc * 128:(dc + 1) * 128],
                         pt3[j % 2][:, 0:128], sem="s_ve", n=1, label=f"p3_te_{i}_{dc}")
                e.wait("s_pe", f"p3_lg_{i}")
                e.op("tensor_copy", lgs[:, i * 8:(i + 1) * 8], ps_lg[i % 2][0:128, 0:8],
                     sem="s_ve", n=1, label=f"p3_lge_{i}")
            for t in range(NTILE):
                lg = lgs[:, t * 8:(t + 1) * 8]
                e.op("tensor_reduce", gtmp["m1"][:, :], lg, AX.X, ALU.max,
                     sem="s_ve", n=1, label=f"p3_ga_{t}")
                e.wait("s_ve", f"p3_ga_{t}")
                if t >= 1:
                    e.wait("s_sa", f"p3_ge_{t-1}")
                e.op("tensor_scalar", gtmp["ls"][:, :], lg, gtmp["m1"][:, 0:1], 0.0,
                     ALU.subtract, ALU.bypass, sem="s_ve", n=1, label=f"p3_gb_{t}")
                e.wait("s_sa", f"p3_ge_{t}")
                e.op("tensor_reduce", gtmp["se"][:, :], gtmp["pe_"][:, :], AX.X, ALU.add,
                     sem="s_ve", n=1, label=f"p3_gc_{t}")
                e.wait("s_ve", f"p3_gc_{t}")
                e.op("reciprocal", gtmp["ri"][:, :], gtmp["se"][:, :],
                     sem="s_ve", n=1, label=f"p3_gd_{t}")
                e.wait("s_ve", f"p3_gd_{t}")
                e.op("tensor_scalar", gtmp["pr_"][:, :], gtmp["pe_"][:, :],
                     gtmp["ri"][:, 0:1], 0.0, ALU.mult, ALU.bypass,
                     sem="s_ve", n=1, label=f"p3_gf_{t}")
                e.wait("s_ve", f"p3_gf_{t}")
                e.op("tensor_reduce", gtmp["m1p"][:, :], gtmp["pr_"][:, :], AX.X, ALU.max,
                     sem="s_ve", n=1, label=f"p3_gg_{t}")
                e.wait("s_ve", f"p3_gg_{t}")
                e.op("tensor_scalar", gtmp["mk"][:, :], gtmp["pr_"][:, :],
                     gtmp["m1p"][:, 0:1], NEG, ALU.is_ge, ALU.mult,
                     sem="s_ve", n=1, label=f"p3_gh_{t}")
                e.wait("s_ve", f"p3_gh_{t}")
                e.op("tensor_tensor", gtmp["pm"][:, :], gtmp["pr_"][:, :], gtmp["mk"][:, :],
                     ALU.add, sem="s_ve", n=1, label=f"p3_gi_{t}")
                e.wait("s_ve", f"p3_gi_{t}")
                e.op("tensor_reduce", gtmp["m2"][:, :], gtmp["pm"][:, :], AX.X, ALU.max,
                     sem="s_ve", n=1, label=f"p3_gj_{t}")
                e.wait("s_ve", f"p3_gj_{t}")
                e.op("tensor_scalar", gtmp["kp"][:, :], gtmp["pr_"][:, :],
                     gtmp["m2"][:, 0:1], 0.0, ALU.is_ge, ALU.bypass,
                     sem="s_ve", n=1, label=f"p3_gk_{t}")
                e.wait("s_ve", f"p3_gk_{t}")
                e.op("tensor_tensor", g_sb[:, t * 8:(t + 1) * 8], gtmp["pr_"][:, :],
                     gtmp["kp"][:, :], ALU.mult, sem="s_ve", n=1, label=f"p3_g_{t}")
            for t in range(NTILE):
                e.wait("s_pe", f"p3_gt_{t}")
                with _lp(nc, e.eng is not None):
                    e.op("tensor_copy", GT[0:8, t * 128:(t + 1) * 128],
                         ps_lg[t % 2][0:8, 0:128], sem="s_ve", n=1, label=f"p3_gte_{t}")

        def p3_scalar(e):
            for dc in range(8):
                e.wait("s_pe", f"p3_gwt_{dc}")
                with _lp(nc, e.eng is not None):
                    e.op("copy", gwT[:, dc * 8:(dc + 1) * 8], ps_lg[dc % 2][0:128, 0:8],
                         sem="s_sa", n=1, label=f"p3_gwe_{dc}")
            for i in range(NTILE):
                p = i % 2
                e.wait("s_ve", f"p3_x1_{i}")
                if i >= 2:
                    e.wait("s_ve", f"p3_vv_{i-2}")
                with _lp(nc, e.eng is not None):
                    e.op("activation", sqd3[:, :], x1t[p][:, :], AF.Square,
                         accum_out=st3["ssq"][:, p:p + 1], sem="s_sa", n=1, label=f"p3_sq_{i}")
                e.op("dma_start", x1sp[i * 128:(i + 1) * 128, :], x1t[p][:, :],
                     sem="d_so", n=16, label=f"p3_x1d_{i}")
                e.wait("s_ve", f"p3_var_{i}")
                e.op("activation", st3["lnv"][:, p:p + 1], st3["var0"][:, p:p + 1], AF.Ln,
                     sem="s_sa", n=1, label=f"p3_ln_{i}")
                e.wait("s_sa", f"p3_ln_{i}")
                e.op("activation", st3["rstd"][:, p:p + 1], st3["lnv"][:, p:p + 1], AF.Exp,
                     scale=-0.5, sem="s_sa", n=1, label=f"p3_rstd_{i}")
                e.wait("s_sa", f"p3_rstd_{i}")
                e.wait("s_ve", f"p3_nb_{i}")
                if i >= 2:
                    e.wait("s_pe", f"p3_tr_{i-2}_7")
                with _lp(nc, e.eng is not None):
                    e.op("activation", h2s[p][:, :], x1t[p][:, :], AF.Identity,
                         bias=st3["nb"][:, p:p + 1], scale=st3["rstd"][:, p:p + 1],
                         sem="s_sa", n=1, label=f"p3_norm_{i}")
                e.wait("s_ve", f"p3_te_{i}_7")
                dst = h2sp.rearrange("(dc p) t -> p dc t", p=128)[:, :, i * 128:(i + 1) * 128]
                src = h2Ts[i % 2][:, :].rearrange("p (dc t) -> p dc t", t=128)
                e.op("dma_start", dst, src, sem="d_so", n=16, label=f"p3_h2d_{i}")
            for t in range(NTILE):
                e.wait("s_ve", f"p3_gb_{t}")
                if t >= 1:
                    e.wait("s_ve", f"p3_gf_{t-1}")
                e.op("activation", gtmp["pe_"][:, :], gtmp["ls"][:, :], AF.Exp,
                     sem="s_sa", n=1, label=f"p3_ge_{t}")
            e.wait("d_so", f"p3_h2d_{NTILE-1}")
            e.wait("d_so", f"p3_x1d_{NTILE-1}")

        run_phase({"gpsimd": p3_gpsimd, "sync": p3_sync, "tensor": p3_tensor,
                   "vector": p3_vector, "scalar": p3_scalar})

        if "g" in TAPS:
            def p3_tap(e):
                e.wait("s_ve", f"p3_g_{NTILE-1}")
                e.op("dma_start", tap_g[:, :], g_sb[:, :], sem="d_x", n=16)
            run_phase({"sync": p3_tap})

        barrier()
        p3.close()

    es_r.close()

    if 4 in PHASES:
        p4 = ExitStack()
        h2q = p4.enter_context(nc.sbuf_tensor("h2q", [128, 8 * TQ], BF16))
        acc = p4.enter_context(nc.sbuf_tensor("acc", [128, 8 * TQ], F32))
        hid = p4.enter_context(nc.sbuf_tensor("hid", [128, 32 * TQ], BF16))
        w2s = p4.enter_context(nc.sbuf_tensor("w2s", [128, 32 * D], BF16))
        w1r = [p4.enter_context(nc.sbuf_tensor(f"w1r{i}", [128, 8 * 256], BF16)) for i in range(2)]
        x1t4 = [p4.enter_context(nc.sbuf_tensor(f"x1t4{i}", [128, D], F32)) for i in range(2)]
        ph4 = [p4.enter_context(nc.psum_tensor(f"ph4{i}", [128, 1024], F32)) for i in range(2)]
        po4 = [p4.enter_context(nc.psum_tensor(f"po4{i}", [128, 1024], F32)) for i in range(2)]

        NCH = 16

        def p4_sync(e):
            e.wait("d_w", "pc_w_255")
            for q in range(NQ):
                if q >= 1:
                    e.wait("s_pe", f"p4_mm1f_{q*8-1}_31")
                src = h2sp.rearrange("(dc p) t -> p dc t", p=128)[:, :, q * TQ:(q + 1) * TQ]
                dst = h2q[:, :].rearrange("p (dc t) -> p dc t", t=TQ)
                e.op("dma_start", dst, src, sem="d_x", n=16, label=f"p4_h2_{q}")
                for ee in range(E):
                    ge = q * 8 + ee
                    for c in range(NCH):
                        cg = ge * NCH + c
                        if cg >= 2:
                            pge, pc = (cg - 2) // NCH, (cg - 2) % NCH
                            e.wait("s_pe", f"p4_mm1f_{pge}_{2*pc+1}")
                        src = bass.AP(w1b.tensor, ee * D * FF + c * 256,
                                      [[FF, 128], [128 * FF, 8], [1, 256]])
                        dst = w1r[cg % 2][:, :].rearrange("p (dc f) -> p dc f", f=256)
                        e.op("dma_start", dst, src, sem="d_x", n=16, label=f"p4_w1_{cg}")
                    if ge >= 1:
                        e.wait("s_pe", f"p4_mm2t_{ge-1}_7")
                    src = bass.AP(w2b.tensor, ee * FF * D,
                                  [[D, 128], [128 * D, 32], [1, D]])
                    dst = w2s[:, :].rearrange("p (fb d) -> p fb d", d=D)
                    e.op("dma_start", dst, src, sem="d_x", n=16, label=f"p4_w2_{ge}")
                for tt in range(8):
                    gt = q * 8 + tt
                    if gt >= 2:
                        e.wait("d_so", f"p4_yd_{(gt-2)//8}_{(gt-2)%8}")
                    e.op("dma_start", x1t4[tt % 2][:, :],
                         x1sp[gt * 128:(gt + 1) * 128, :], sem="d_x", n=16,
                         label=f"p4_x1_{q}_{tt}")

        def p4_tensor(e):
            for q in range(NQ):
                for ee in range(E):
                    ge = q * 8 + ee
                    e.wait("d_x", f"p4_h2_{q}")
                    for c in range(NCH):
                        cg = ge * NCH + c
                        e.wait("d_x", f"p4_w1_{cg}")
                        for fbl in range(2):
                            fb = 2 * c + fbl
                            if fb >= 2:
                                e.wait("s_sa", f"p4_relu_{ge}_{fb-2}")
                            elif ge >= 1:
                                e.wait("s_sa", f"p4_relu_{ge-1}_{fb+30}")
                            for dc in range(8):
                                for th in range(2):
                                    e.op("matmul", ph4[fb % 2][:, th * 512:(th + 1) * 512],
                                         w1r[cg % 2][:, dc * 256 + fbl * 128: dc * 256 + fbl * 128 + 128],
                                         h2q[:, dc * TQ + th * 512: dc * TQ + (th + 1) * 512],
                                         start=(dc == 0), stop=(dc == 7),
                                         sem=("s_pe" if (dc == 7 and th == 1) else None), n=1,
                                         label=(f"p4_mm1f_{ge}_{fb}" if (dc == 7 and th == 1) else None))
                    e.wait("d_x", f"p4_w2_{ge}")
                    e.wait("s_sa", f"p4_relu_{ge}_31")
                    for tt in range(8):
                        if tt >= 2:
                            e.wait("s_ve", f"p4_acc_{ge}_{tt-2}")
                        elif ge % 8 == 0 and ge >= 1:
                            e.wait("s_ve", f"p4_cb_{ge//8-1}_{tt+6}")
                        elif ge >= 1:
                            e.wait("s_ve", f"p4_acc_{ge-1}_{tt+6}")
                        for fb in range(32):
                            for dh in range(2):
                                e.op("matmul", po4[tt % 2][:, dh * 512:(dh + 1) * 512],
                                     hid[:, fb * TQ + tt * 128: fb * TQ + (tt + 1) * 128],
                                     w2s[:, fb * D + dh * 512: fb * D + (dh + 1) * 512],
                                     start=(fb == 0), stop=(fb == 31),
                                     sem=("s_pe" if (fb == 31 and dh == 1) else None), n=1,
                                     label=(f"p4_mm2t_{ge}_{tt}" if (fb == 31 and dh == 1) else None))
                if q == 0:
                    e.wait("s_ve", f"p3_gte_{NTILE-1}")
                    e.wait("d_w", "p3_b2")
                for tt in range(8):
                    ge = q * 8 + 7
                    if tt >= 2:
                        e.wait("s_ve", f"p4_cb_{q}_{tt-2}")
                    else:
                        e.wait("s_ve", f"p4_acc_{ge}_{tt+6}")
                    e.wait("s_ve", f"p4_acc_{ge}_{tt}")
                    for dh in range(2):
                        e.op("matmul", po4[tt % 2][:, dh * 512:(dh + 1) * 512],
                             GT[0:8, (q * 8 + tt) * 128:(q * 8 + tt + 1) * 128],
                             b2s[0:8, dh * 512:(dh + 1) * 512],
                             start=True, stop=True,
                             sem=("s_pe" if dh == 1 else None), n=1,
                             label=(f"p4_fc_{q}_{tt}" if dh == 1 else None))

        def p4_scalar(e):
            e.wait("s_ve", f"p3_g_{NTILE-1}")
            e.wait("s_ve", f"p3_b1e_{E-1}")
            for q in range(NQ):
                for ee in range(E):
                    ge = q * 8 + ee
                    for fb in range(32):
                        e.wait("s_pe", f"p4_mm1f_{ge}_{fb}")
                        with _lp(nc, e.eng is not None):
                            e.op("activation", hid[:, fb * TQ:(fb + 1) * TQ], ph4[fb % 2][:, :],
                                 AF.Relu, bias=b1T[:, ee * 32 + fb: ee * 32 + fb + 1],
                                 sem="s_sa", n=1, label=f"p4_relu_{ge}_{fb}")
                    for tt in range(8):
                        e.wait("s_pe", f"p4_mm2t_{ge}_{tt}")
                        e.op("activation", po4[tt % 2][:, :], po4[tt % 2][:, :], AF.Copy,
                             scale=g_sb[:, (q * 8 + tt) * 8 + ee: (q * 8 + tt) * 8 + ee + 1],
                             sem="s_sa", n=1, label=f"p4_sc_{ge}_{tt}")
                for tt in range(8):
                    e.wait("s_ve", f"p4_y_{q}_{tt}")
                    e.op("dma_start", y_d[(q * 8 + tt) * 128:(q * 8 + tt + 1) * 128, :],
                         x1t4[tt % 2][:, :], sem="d_so", n=16, label=f"p4_yd_{q}_{tt}")
            e.wait("d_so", f"p4_yd_{NQ-1}_7")

        def p4_vector(e):
            for q in range(NQ):
                e.op("memset", acc[:, :], 0.0, sem="s_ve", n=1, label=f"p4_ms_{q}")
                for ee in range(E):
                    ge = q * 8 + ee
                    for tt in range(8):
                        e.wait("s_sa", f"p4_sc_{ge}_{tt}")
                        if ee == 0:
                            e.wait("s_ve", f"p4_ms_{q}")
                        e.op("tensor_tensor", acc[:, tt * D:(tt + 1) * D],
                             po4[tt % 2][:, :], acc[:, tt * D:(tt + 1) * D], ALU.add,
                             sem="s_ve", n=1, label=f"p4_acc_{ge}_{tt}")
                for tt in range(8):
                    e.wait("s_pe", f"p4_fc_{q}_{tt}")
                    e.wait("d_x", f"p4_x1_{q}_{tt}")
                    e.op("tensor_tensor", x1t4[tt % 2][:, :], po4[tt % 2][:, :],
                         x1t4[tt % 2][:, :], ALU.add, sem="s_ve", n=1, label=f"p4_cb_{q}_{tt}")
                    e.wait("s_ve", f"p4_cb_{q}_{tt}")
                    e.op("tensor_tensor", x1t4[tt % 2][:, :], x1t4[tt % 2][:, :],
                         acc[:, tt * D:(tt + 1) * D], ALU.add,
                         sem="s_ve", n=1, label=f"p4_y_{q}_{tt}")

        run_phase({"sync": p4_sync, "tensor": p4_tensor,
                   "scalar": p4_scalar, "vector": p4_vector})
        p4.close()

    es.close()
    return nc


_NC_CACHE = {}


def _get_nc():
    key = (tuple(sorted(TAPS)), tuple(PHASES))
    if key not in _NC_CACHE:
        _NC_CACHE[key] = build_nc()
    return _NC_CACHE[key]


def _consts():
    bf = ml_dtypes.bfloat16
    idb = np.eye(128, dtype=np.float32).astype(bf)
    idf = np.eye(128, dtype=np.float32)
    p = np.arange(128)[:, None]
    c = np.arange(896)[None, :]
    maskc = np.where(p <= c - 384, 0.0, NEG).astype(np.float32)
    onesb = np.ones((1, 128), dtype=np.float32).astype(bf)
    return {"idb": idb, "idf": idf, "maskc": maskc, "onesb": onesb}


def kernel(L0_attn_x, L0_attn_Wq, L0_attn_Wk, L0_attn_Wv, L0_attn_Wo, L0_attn_bo,
           L0_ffn_gate_w, L0_ffn_w1, L0_ffn_b1, L0_ffn_w2, L0_ffn_b2):
    nc = _get_nc()
    consts = _consts()
    base = {
        "wq": np.ascontiguousarray(L0_attn_Wq, dtype=np.float32),
        "wk": np.ascontiguousarray(L0_attn_Wk, dtype=np.float32),
        "wv": np.ascontiguousarray(L0_attn_Wv, dtype=np.float32),
        "wo": np.ascontiguousarray(L0_attn_Wo, dtype=np.float32),
        "bo": np.ascontiguousarray(L0_attn_bo, dtype=np.float32).reshape(1, D),
        "gw": np.ascontiguousarray(L0_ffn_gate_w, dtype=np.float32),
        "w1": np.ascontiguousarray(L0_ffn_w1, dtype=np.float32),
        "b1": np.ascontiguousarray(L0_ffn_b1, dtype=np.float32),
        "w2": np.ascontiguousarray(L0_ffn_w2, dtype=np.float32),
        "b2": np.ascontiguousarray(L0_ffn_b2, dtype=np.float32),
    }
    base.update(consts)
    xs = np.asarray(L0_attn_x, dtype=np.float32)
    in_maps = []
    for b in range(8):
        m = dict(base)
        m["x"] = np.ascontiguousarray(xs[b])
        in_maps.append(m)
    res = run_bass_kernel_spmd(nc, in_maps, core_ids=list(range(8)))
    out = np.stack([res.results[b]["y"] for b in range(8)], axis=0)
    return out.astype(np.float32)


def run_debug(inputs, taps, phases=(1, 2, 3, 4), cores=(0,), p2_heads=None, p2_qc=None,
              p2_dbg=None):
    global TAPS, PHASES, P2_HEADS, P2_QC, P2_DBG
    if p2_dbg is not None:
        P2_DBG = p2_dbg
    TAPS = set(taps)
    PHASES = tuple(phases)
    if p2_heads is not None:
        P2_HEADS = p2_heads
    if p2_qc is not None:
        P2_QC = p2_qc
    _NC_CACHE.clear()
    nc = _get_nc()
    consts = _consts()
    base = {
        "wq": np.ascontiguousarray(inputs["L0_attn_Wq"], dtype=np.float32),
        "wk": np.ascontiguousarray(inputs["L0_attn_Wk"], dtype=np.float32),
        "wv": np.ascontiguousarray(inputs["L0_attn_Wv"], dtype=np.float32),
        "wo": np.ascontiguousarray(inputs["L0_attn_Wo"], dtype=np.float32),
        "bo": np.ascontiguousarray(inputs["L0_attn_bo"], dtype=np.float32).reshape(1, D),
        "gw": np.ascontiguousarray(inputs["L0_ffn_gate_w"], dtype=np.float32),
        "w1": np.ascontiguousarray(inputs["L0_ffn_w1"], dtype=np.float32),
        "b1": np.ascontiguousarray(inputs["L0_ffn_b1"], dtype=np.float32),
        "w2": np.ascontiguousarray(inputs["L0_ffn_w2"], dtype=np.float32),
        "b2": np.ascontiguousarray(inputs["L0_ffn_b2"], dtype=np.float32),
    }
    base.update(consts)
    xs = np.asarray(inputs["L0_attn_x"], dtype=np.float32)
    in_maps = []
    for b in cores:
        m = dict(base)
        m["x"] = np.ascontiguousarray(xs[b])
        in_maps.append(m)
    res = run_bass_kernel_spmd(nc, in_maps, core_ids=list(cores))
    TAPS = set()
    PHASES = (1, 2, 3, 4)
    P2_HEADS = H
    P2_QC = 8
    _NC_CACHE.clear()
    return res.results


# version 09 (final)
# speedup vs baseline: 1.0170x; 1.0022x over previous
import numpy as np
import ml_dtypes
from contextlib import ExitStack

import concourse.bass as bass
import concourse.mybir as mybir
from concourse.bass_utils import run_bass_kernel_spmd

F32 = mybir.dt.float32
BF16 = mybir.dt.bfloat16
AF = mybir.ActivationFunctionType
ALU = mybir.AluOpType
AX = mybir.AxisListType

T = 4096
D = 1024
NTILE = 32
H = 16
HD = 64
E = 8
FF = 4096
NQ = 4
TQ = 1024
QC = 512
C_SHIFT = -32.0
EPS = 1e-5
QSCALE = 0.125
NEG = -1.0e30

TAPS = set()
PHASES = (1, 2, 3, 4)
P2_HEADS = H
P2_QC = 8
P2_DBG = 0

ENGS = ("tensor", "vector", "scalar", "gpsimd", "sync")


class Sched:

    def __init__(self):
        self.marks = {}
        self._c1 = {}
        self._c2 = {}
        self.counts = self._c1
        self.pass2 = False
        self.missing = set()

    def begin_pass(self, pass2):
        self.pass2 = pass2
        self.counts = self._c2 if pass2 else self._c1

    def inc(self, sem, n, label=None):
        c = self.counts.get(sem, 0) + n
        self.counts[sem] = c
        if label is not None:
            if self.pass2:
                assert self.marks.get(label) == c, (label, self.marks.get(label), c)
            else:
                assert label not in self.marks, f"dup label {label}"
                self.marks[label] = c
        return c

    def at(self, label):
        if label in self.marks:
            return self.marks[label]
        assert not self.pass2, f"unknown label {label}"
        self.missing.add(label)
        return 0


class Eng:

    def __init__(self, S, sems, eng=None):
        self.S = S
        self.sems = sems
        self.eng = eng

    def wait(self, sem, label):
        v = self.S.at(label)
        if self.eng is not None and v > 0:
            self.eng.wait_ge(self.sems[sem], v)

    def wait_val(self, sem, v):
        if self.eng is not None and v > 0:
            self.eng.wait_ge(self.sems[sem], v)

    def op(self, fname, *a, sem=None, n=1, label=None, **k):
        if self.eng is not None:
            inst = getattr(self.eng, fname)(*a, **k)
            if sem is not None:
                inst.then_inc(self.sems[sem], n)
        if sem is not None:
            self.S.inc(sem, n, label)


def _lp(nc, real):
    if real:
        return nc.allow_low_precision(reason="bf16 matmul operand rounding")
    import contextlib
    return contextlib.nullcontext()


def build_nc():
    nc = bass.Bass("TRN2", target_bir_lowering=False)

    x_d = nc.dram_tensor("x", [T, D], F32, kind="ExternalInput").ap()
    wq_d = nc.dram_tensor("wq", [D, D], F32, kind="ExternalInput").ap()
    wk_d = nc.dram_tensor("wk", [D, D], F32, kind="ExternalInput").ap()
    wv_d = nc.dram_tensor("wv", [D, D], F32, kind="ExternalInput").ap()
    wo_d = nc.dram_tensor("wo", [D, D], F32, kind="ExternalInput").ap()
    bo_d = nc.dram_tensor("bo", [1, D], F32, kind="ExternalInput").ap()
    gw_d = nc.dram_tensor("gw", [E, D], F32, kind="ExternalInput").ap()
    w1_d = nc.dram_tensor("w1", [E, D, FF], F32, kind="ExternalInput").ap()
    b1_d = nc.dram_tensor("b1", [E, FF], F32, kind="ExternalInput").ap()
    w2_d = nc.dram_tensor("w2", [E, FF, D], F32, kind="ExternalInput").ap()
    b2_d = nc.dram_tensor("b2", [E, D], F32, kind="ExternalInput").ap()
    idb_d = nc.dram_tensor("idb", [128, 128], BF16, kind="ExternalInput").ap()
    idf_d = nc.dram_tensor("idf", [128, 128], F32, kind="ExternalInput").ap()
    mask_d = nc.dram_tensor("maskc", [128, 896], F32, kind="ExternalInput").ap()
    ones_d = nc.dram_tensor("onesb", [1, 128], BF16, kind="ExternalInput").ap()

    y_d = nc.dram_tensor("y", [T, D], F32, kind="ExternalOutput").ap()

    def scratch(name, shape, dt, tap):
        kind = "ExternalOutput" if tap in TAPS else "Internal"
        return nc.dram_tensor(name, shape, dt, kind=kind).ap()

    _wk = "ExternalOutput" if "wimg" in TAPS else "Internal"
    w1b = nc.dram_tensor("w1b", [E, D, FF], BF16, kind=_wk).ap()
    w2b = nc.dram_tensor("w2b", [E, FF, D], BF16, kind=_wk).ap()
    qsp = scratch("qsp", [D, T], BF16, "qsp")
    ksp = scratch("ksp", [D, T], BF16, "ksp")
    x1sp = scratch("x1sp", [T, D], F32, "x1sp")
    h2sp = scratch("h2sp", [D, T], BF16, "h2sp")
    tap_h1T = scratch("tap_h1T", [128, 8 * T], BF16, "h1T") if "h1T" in TAPS else None
    tap_vaug = scratch("tap_vaug", [128, H * 32 * 65], BF16, "vaug") if "vaug" in TAPS else None
    tap_oT = scratch("tap_oT", [128, 8 * T], BF16, "oT") if "oT" in TAPS else None
    tap_g = scratch("tap_g", [128, NTILE * E], F32, "g") if "g" in TAPS else None

    S = Sched()
    es = ExitStack()

    sems = {}
    for s in ("d_x", "d_so", "d_w", "s_pe", "s_sa", "s_ve", "s_gp", "s_bar"):
        sems[s] = es.enter_context(nc.semaphore(s))

    idb = es.enter_context(nc.sbuf_tensor("idb_s", [128, 128], BF16))
    idf = es.enter_context(nc.sbuf_tensor("idf_s", [128, 128], F32))
    onesb = es.enter_context(nc.sbuf_tensor("ones_s", [1, 128], BF16))
    g_sb = es.enter_context(nc.sbuf_tensor("g_sb", [128, NTILE * E], F32))
    GT = es.enter_context(nc.sbuf_tensor("GT", [8, T], BF16))
    b2s = es.enter_context(nc.sbuf_tensor("b2s", [8, D], BF16))
    b1T = es.enter_context(nc.sbuf_tensor("b1T", [128, E * 32], F32))
    cm32 = es.enter_context(nc.sbuf_tensor("cm32", [128, 1], F32))

    es_r = ExitStack()
    vaug = es_r.enter_context(nc.sbuf_tensor("vaug", [128, H * 32 * 65], BF16, side="right"))

    barrier_n = [0]

    def run_phase(fns):
        S.begin_pass(False)
        for en in ENGS:
            if en in fns:
                fns[en](Eng(S, sems))
        missing = S.missing - set(S.marks)
        assert not missing, f"labels never defined: {sorted(missing)[:10]}"
        S.missing = set()
        S.begin_pass(True)
        with nc.Block() as blk:
            for en in ENGS:
                if en in fns:
                    getattr(blk, en)(lambda eng, fn=fns[en]: fn(Eng(S, sems, eng)))

    def barrier():
        barrier_n[0] += 1
        tgt = 5 * barrier_n[0]

        def mk(en):
            def f(e):
                e.op("nop", sem="s_bar", n=1)
                e.wait_val("s_bar", tgt)
            return f
        run_phase({en: mk(en) for en in ENGS})

    def p0_gpsimd(e):
        e.op("memset", cm32[:, :], C_SHIFT, sem="s_gp", n=1, label="c_m32")

    def p0_sync(e):
        e.op("dma_start", idb[:, :], idb_d[:, :], sem="d_x", n=16, label="c_idb")
        e.op("dma_start", idf[:, :], idf_d[:, :], sem="d_x", n=16, label="c_idf")
        e.op("dma_start", onesb[:, :], ones_d[:, :], sem="d_x", n=16, label="c_ones")

    run_phase({"sync": p0_sync, "gpsimd": p0_gpsimd})

    if 1 in PHASES:
        p1 = ExitStack()
        xt = [p1.enter_context(nc.sbuf_tensor(f"xt{i}", [128, D], F32)) for i in range(3)]
        h1s = [p1.enter_context(nc.sbuf_tensor(f"h1s{i}", [128, D], BF16)) for i in range(2)]
        sqd = p1.enter_context(nc.sbuf_tensor("sqd", [128, D], BF16))
        st = {}
        for nm in ("ssum", "ssq", "mm_", "vv", "m2v", "var0", "lnv", "rstd", "nb"):
            st[nm] = p1.enter_context(nc.sbuf_tensor(nm, [128, 2], F32))
        h1T = p1.enter_context(nc.sbuf_tensor("h1T", [128, 8 * T], BF16))
        wbuf = [p1.enter_context(nc.sbuf_tensor(f"wbuf{i}", [128, 8 * D], BF16))
                for i in range(2)]
        wqs, wks, wvs = wbuf[0], wbuf[1], wbuf[0]
        qks = [p1.enter_context(nc.sbuf_tensor(f"qks{i}", [128, 512], BF16)) for i in range(4)]
        pt = [p1.enter_context(nc.psum_tensor(f"pt{i}", [128, 1024], BF16)) for i in range(2)]
        psq = [p1.enter_context(nc.psum_tensor(f"psq{i}", [128, 512], F32)) for i in range(2)]
        psv = [p1.enter_context(nc.psum_tensor(f"psv{i}", [128, 512], F32)) for i in range(2)]

        def p1_gpsimd(e):
            for nm, sb, dr in (("q", wqs, wq_d), ("k", wks, wk_d)):
                src = dr.rearrange("(dc p) c -> p dc c", p=128)
                dst = sb[:, :].rearrange("p (dc c) -> p dc c", c=D)
                e.op("dma_start", dst, src, sem="d_w", n=16, label=f"p1_w{nm}")
            e.wait("s_pe", "p1_qk_0_7_7")
            src = wv_d.rearrange("(dc p) c -> p dc c", p=128)
            dst = wvs[:, :].rearrange("p (dc c) -> p dc c", c=D)
            e.op("dma_start", dst, src, sem="d_w", n=16, label="p1_wv")
            va = bass.AP(vaug, HD, [[vaug.shape[1], 128], [65, H * 32], [1, 1]])
            e.op("memset", va, 1.0, sem="s_gp", n=1, label="p1_vones")

        def p1_sync(e):
            for i in range(NTILE):
                if i >= 3:
                    e.wait("s_sa", f"p1_norm_{i-3}")
                e.op("dma_start", xt[i % 3][:, :], x_d[i * 128:(i + 1) * 128, :],
                     sem="d_x", n=16, label=f"p1_x_{i}")

        def p1_vector(e):
            inv_d = 1.0 / D
            for i in range(NTILE):
                p = i % 2
                e.wait("d_x", f"p1_x_{i}")
                e.op("tensor_reduce", st["ssum"][:, p:p + 1], xt[i % 3][:, :], AX.X, ALU.add,
                     sem="s_ve", n=1, label=f"p1_sum_{i}")
                e.wait("s_ve", f"p1_sum_{i}")
                e.op("tensor_scalar", st["mm_"][:, p:p + 1], st["ssum"][:, p:p + 1],
                     inv_d, 0.0, ALU.mult, ALU.bypass, sem="s_ve", n=1, label=f"p1_m_{i}")
                e.wait("s_ve", f"p1_m_{i}")
                e.wait("s_sa", f"p1_sq_{i}")
                e.op("tensor_scalar", st["vv"][:, p:p + 1], st["ssq"][:, p:p + 1],
                     inv_d, EPS, ALU.mult, ALU.add, sem="s_ve", n=1, label=f"p1_vv_{i}")
                e.op("tensor_tensor", st["m2v"][:, p:p + 1], st["mm_"][:, p:p + 1],
                     st["mm_"][:, p:p + 1], ALU.mult, sem="s_ve", n=1, label=f"p1_m2_{i}")
                e.wait("s_ve", f"p1_m2_{i}")
                e.op("tensor_tensor", st["var0"][:, p:p + 1], st["vv"][:, p:p + 1],
                     st["m2v"][:, p:p + 1], ALU.subtract, sem="s_ve", n=1, label=f"p1_var_{i}")
                e.wait("s_sa", f"p1_rstd_{i}")
                e.op("tensor_scalar", st["nb"][:, p:p + 1], st["mm_"][:, p:p + 1],
                     st["rstd"][:, p:p + 1], -1.0, ALU.mult, ALU.mult,
                     sem="s_ve", n=1, label=f"p1_nb_{i}")
                if i > 0:
                    for dc in range(8):
                        j = (i - 1) * 8 + dc
                        e.wait("s_pe", f"p1_tr_{i-1}_{dc}")
                        e.op("tensor_copy", h1T[:, dc * T + (i - 1) * 128: dc * T + i * 128],
                             pt[j % 2][:, 0:128], sem="s_ve", n=1, label=f"p1_te_{i-1}_{dc}")
            for dc in range(8):
                j = (NTILE - 1) * 8 + dc
                e.wait("s_pe", f"p1_tr_{NTILE-1}_{dc}")
                e.op("tensor_copy", h1T[:, dc * T + (NTILE - 1) * 128: dc * T + NTILE * 128],
                     pt[j % 2][:, 0:128], sem="s_ve", n=1, label=f"p1_te_{NTILE-1}_{dc}")
            for tt in range(NTILE):
                for dh in range(2):
                    j = tt * 2 + dh
                    e.wait("s_pe", f"p1_v_{tt}_{dh}")
                    dst = bass.AP(vaug, ((dh * 8) * 32 + tt) * 65,
                                  [[vaug.shape[1], 128], [32 * 65, 8], [1, 64]])
                    src = psv[j % 2][:, :].rearrange("p (h c) -> p h c", c=64)
                    with _lp(nc, e.eng is not None):
                        e.op("tensor_copy", dst, src, sem="s_ve", n=1, label=f"p1_ve_{tt}_{dh}")

        def p1_scalar(e):
            for i in range(NTILE):
                p = i % 2
                e.wait("d_x", f"p1_x_{i}")
                if i >= 2:
                    e.wait("s_sa", f"p1_sq_{i-2}")
                    e.wait("s_ve", f"p1_vv_{i-2}")
                with _lp(nc, e.eng is not None):
                    e.op("activation", sqd[:, :], xt[i % 3][:, :], AF.Square,
                         accum_out=st["ssq"][:, p:p + 1], sem="s_sa", n=1, label=f"p1_sq_{i}")
                e.wait("s_ve", f"p1_var_{i}")
                e.op("activation", st["lnv"][:, p:p + 1], st["var0"][:, p:p + 1], AF.Ln,
                     sem="s_sa", n=1, label=f"p1_ln_{i}")
                e.wait("s_sa", f"p1_ln_{i}")
                e.op("activation", st["rstd"][:, p:p + 1], st["lnv"][:, p:p + 1], AF.Exp,
                     scale=-0.5, sem="s_sa", n=1, label=f"p1_rstd_{i}")
                e.wait("s_sa", f"p1_rstd_{i}")
                e.wait("s_ve", f"p1_nb_{i}")
                if i >= 2:
                    e.wait("s_pe", f"p1_tr_{i-2}_7")
                with _lp(nc, e.eng is not None):
                    e.op("activation", h1s[p][:, :], xt[i % 3][:, :], AF.Identity,
                         bias=st["nb"][:, p:p + 1], scale=st["rstd"][:, p:p + 1],
                         sem="s_sa", n=1, label=f"p1_norm_{i}")
            for kk, (sc, spd) in enumerate(((QSCALE, qsp), (1.0, ksp))):
                for dt in range(8):
                    for tc in range(8):
                        j = (kk * 8 + dt) * 8 + tc
                        e.wait("s_pe", f"p1_qk_{kk}_{dt}_{tc}")
                        if j >= 4:
                            e.wait("d_so", f"p1_qkd_{j-4}")
                        with _lp(nc, e.eng is not None):
                            e.op("activation", qks[j % 4][:, :], psq[j % 2][:, :], AF.Copy,
                                 scale=sc, sem="s_sa", n=1, label=f"p1_qke_{j}")
                        e.wait("s_sa", f"p1_qke_{j}")
                        e.op("dma_start", spd[dt * 128:(dt + 1) * 128, tc * 512:(tc + 1) * 512],
                             qks[j % 4][:, :], sem="d_so", n=16, label=f"p1_qkd_{j}")
            e.wait("d_so", f"p1_qkd_{127}")

        def p1_tensor(e):
            e.wait("d_x", "c_idb")
            for i in range(NTILE):
                e.wait("s_sa", f"p1_norm_{i}")
                for dc in range(8):
                    j = i * 8 + dc
                    if j >= 2:
                        e.wait("s_ve", f"p1_te_{(j-2)//8}_{(j-2)%8}")
                    e.op("transpose", pt[j % 2][:, 0:128], h1s[i % 2][:, dc * 128:(dc + 1) * 128],
                         idb[:, :], sem="s_pe", n=1, label=f"p1_tr_{i}_{dc}")
            e.wait("d_w", "p1_wq")
            e.wait("d_w", "p1_wk")
            for kk, wsb in enumerate((wqs, wks)):
                for dt in range(8):
                    for tc in range(8):
                        j = (kk * 8 + dt) * 8 + tc
                        if j >= 2:
                            e.wait("s_sa", f"p1_qke_{j-2}")
                        for dc in range(8):
                            e.op("matmul", psq[j % 2][:, :],
                                 wsb[:, dc * D + dt * 128: dc * D + (dt + 1) * 128],
                                 h1T[:, dc * T + tc * 512: dc * T + (tc + 1) * 512],
                                 start=(dc == 0), stop=(dc == 7),
                                 sem=("s_pe" if dc == 7 else None), n=1,
                                 label=(f"p1_qk_{kk}_{dt}_{tc}" if dc == 7 else None))
            e.wait("s_gp", "p1_vones")
            e.wait("d_w", "p1_wv")
            for tt in range(NTILE):
                for dh in range(2):
                    j = tt * 2 + dh
                    if j >= 2:
                        e.wait("s_ve", f"p1_ve_{(j-2)//2}_{(j-2)%2}")
                    for dc in range(8):
                        e.op("matmul", psv[j % 2][:, :],
                             h1T[:, dc * T + tt * 128: dc * T + (tt + 1) * 128],
                             wvs[:, dc * D + dh * 512: dc * D + (dh + 1) * 512],
                             start=(dc == 0), stop=(dc == 7),
                             sem=("s_pe" if dc == 7 else None), n=1,
                             label=(f"p1_v_{tt}_{dh}" if dc == 7 else None))

        run_phase({"gpsimd": p1_gpsimd, "sync": p1_sync, "vector": p1_vector,
                   "scalar": p1_scalar, "tensor": p1_tensor})

        if "h1T" in TAPS or "vaug" in TAPS:
            def p1_tap(e):
                e.wait("s_ve", f"p1_te_{NTILE-1}_7")
                e.wait("s_ve", f"p1_ve_{NTILE-1}_1")
                last = None
                if tap_h1T is not None:
                    e.op("dma_start", tap_h1T[:, :], h1T[:, :], sem="d_x", n=16,
                         label="p1_tapd1")
                    last = "p1_tapd1"
                if tap_vaug is not None:
                    e.op("dma_start", tap_vaug[:, :], vaug[:, :], sem="d_x", n=16,
                         label="p1_tapd2")
                    last = "p1_tapd2"
                e.wait("d_x", last)
            run_phase({"sync": p1_tap})

        barrier()
        p1.close()

    oT = es_r.enter_context(nc.sbuf_tensor("oT", [128, 8 * T], BF16, side="right"))
    if 2 in PHASES:
        p2 = ExitStack()
        qh = [p2.enter_context(nc.sbuf_tensor(f"qh{i}", [128, T], BF16)) for i in range(2)]
        kh = [p2.enter_context(nc.sbuf_tensor(f"kh{i}", [128, T], BF16)) for i in range(2)]
        pbuf = [p2.enter_context(nc.sbuf_tensor(f"pbuf{i}", [128, 512], BF16)) for i in range(4)]
        stmp = [p2.enter_context(nc.sbuf_tensor(f"stmp{i}", [128, 512], F32)) for i in range(2)]
        masks = p2.enter_context(nc.sbuf_tensor("masks", [128, 896], F32))
        linvf = p2.enter_context(nc.sbuf_tensor("linvf", [1, 512], F32))
        linvb = p2.enter_context(nc.sbuf_tensor("linvb", [1, 512], BF16))
        linvs = [p2.enter_context(nc.sbuf_tensor(f"linvs{i}", [64, 512], F32)) for i in range(2)]
        pcs = [p2.enter_context(nc.sbuf_tensor(f"pcs{i}", [128, 2048], BF16)) for i in range(2)]
        ps_s = [p2.enter_context(nc.psum_tensor(f"ps_s{i}", [128, 512], F32)) for i in range(4)]
        ps_o = [p2.enter_context(nc.psum_tensor(f"ps_o{i}", [65, 512], F32)) for i in range(2)]
        ps_l = [p2.enter_context(nc.psum_tensor(f"ps_l{i}", [64, 512], F32)) for i in range(2)]

        def p2_iter():
            gs = 0
            gm = 0
            for h in range(P2_HEADS):
                for qc in range(P2_QC):
                    nstrip = 4 * (qc + 1)
                    strips = []
                    for kt in range(nstrip):
                        masked = kt >= nstrip - 4
                        strips.append((kt, gs, gm if masked else None, masked))
                        gs += 1
                        if masked:
                            gm += 1
                    yield h, qc, nstrip, strips

        P2IT = list(p2_iter())
        GS_OF_M = {}
        for h, qc, nstrip, strips in P2IT:
            for kt, gs, gm, masked in strips:
                if masked:
                    GS_OF_M[gm] = gs

        def p2_gpsimd(e):
            for i in range(2):
                e.op("memset", qh[i][64:128, :], 0.0, sem="s_gp", n=1, label=f"p2_z_q{i}")
                e.op("memset", kh[i][64:128, :], 0.0, sem="s_gp", n=1, label=f"p2_z_k{i}")
            j = 0
            for ee in range(E):
                for blk in range(16):
                    if j >= 2:
                        e.wait("d_w", f"pc_w_{j-2}")
                    r0, fh = (blk // 2) * 128, (blk % 2) * 2048
                    e.op("dma_start", pcs[j % 2][:, :],
                         w1_d[ee, r0:r0 + 128, fh:fh + 2048],
                         max_dma_last_dim=8192, sem="d_w", n=16, label=f"pc_r_{j}")
                    e.wait("d_w", f"pc_r_{j}")
                    e.op("dma_start", w1b[ee, r0:r0 + 128, fh:fh + 2048],
                         pcs[j % 2][:, :], sem="d_w", n=16, label=f"pc_w_{j}")
                    j += 1
                for blk in range(16):
                    if j >= 2:
                        e.wait("d_w", f"pc_w_{j-2}")
                    src = bass.AP(w2_d.tensor, ee * FF * D + blk * 256 * D,
                                  [[D, 128], [128 * D, 2], [1, D]])
                    e.op("dma_start", pcs[j % 2][:, :].rearrange("p (r c) -> p r c", c=D),
                         src, max_dma_last_dim=8192, sem="d_w", n=16, label=f"pc_r_{j}")
                    e.wait("d_w", f"pc_r_{j}")
                    dst = bass.AP(w2b.tensor, ee * FF * D + blk * 256 * D,
                                  [[D, 128], [128 * D, 2], [1, D]])
                    e.op("dma_start", dst, pcs[j % 2][:, :].rearrange("p (r c) -> p r c", c=D),
                         sem="d_w", n=16, label=f"pc_w_{j}")
                    j += 1

        def p2_sync(e):
            e.op("dma_start", masks[:, :], mask_d[:, :], sem="d_x", n=16, label="p2_mask")
            last_gp_of_head = {}
            for hh, qq, npp, prs in P2IT:
                last_gp_of_head[hh] = prs[-1][1]
            for h in range(P2_HEADS):
                if h >= 2:
                    e.wait("s_pe", f"p2_pv_{last_gp_of_head[h-2]}")
                e.op("dma_start", qh[h % 2][0:64, :], qsp[h * 64:(h + 1) * 64, :],
                     sem="d_x", n=16, label=f"p2_q_{h}")
                e.op("dma_start", kh[h % 2][0:64, :], ksp[h * 64:(h + 1) * 64, :],
                     sem="d_x", n=16, label=f"p2_k_{h}")

        def p2_tensor(e):
            if 1 in PHASES:
                e.wait("s_ve", f"p1_ve_{NTILE-1}_1")
            e.wait("s_gp", "p2_z_k1")
            pend = None
            for h, qc, nstrip, strips in P2IT:
                idx = h * 8 + qc
                op_ = idx % 2
                e.wait("d_x", f"p2_q_{h}")
                e.wait("d_x", f"p2_k_{h}")
                def emit_s(kt, gs):
                    e.op("matmul", ps_s[gs % 4][:, :],
                         kh[h % 2][:, kt * 128:(kt + 1) * 128],
                         qh[h % 2][:, qc * 512:(qc + 1) * 512],
                         start=True, stop=True, sem="s_pe", n=1, label=f"p2_s_{gs}")

                def emit_pv(kt, gs):
                    e.wait("s_sa", f"p2_exp_{gs}")
                    e.op("matmul", ps_o[op_][:, :],
                         vaug[:, (h * 32 + kt) * 65: (h * 32 + kt) * 65 + 65],
                         pbuf[gs % 4][:, :],
                         start=(kt == 0), stop=(kt == nstrip - 1),
                         sem="s_pe", n=1, label=f"p2_pv_{gs}")

                DEPTH = 3
                lead = min(DEPTH, nstrip)
                for j in range(lead):
                    emit_s(strips[j][0], strips[j][1])
                if pend is not None:
                    pidx = pend
                    e.wait("s_ve", f"p2_linv_{pidx}")
                    if pidx >= 2:
                        e.wait("s_sa", f"p2_lc_{pidx-2}")
                    e.op("matmul", ps_l[pidx % 2][:, :], onesb[0:1, 0:64],
                         linvb[0:1, :], start=True, stop=True,
                         sem="s_pe", n=1, label=f"p2_lb_{pidx}")
                    pend = None
                if idx >= 2:
                    e.wait("s_ve", f"p2_otev_{idx-2}")
                for j in range(lead, nstrip):
                    emit_s(strips[j][0], strips[j][1])
                    emit_pv(strips[j - lead][0], strips[j - lead][1])
                for j in range(nstrip - lead, nstrip):
                    emit_pv(strips[j][0], strips[j][1])
                pend = idx
            pidx = pend
            e.wait("s_ve", f"p2_linv_{pidx}")
            if pidx >= 2:
                e.wait("s_sa", f"p2_lc_{pidx-2}")
            e.op("matmul", ps_l[pidx % 2][:, :], onesb[0:1, 0:64], linvb[0:1, :],
                 start=True, stop=True, sem="s_pe", n=1, label=f"p2_lb_{pidx}")

        def p2_vector(e):
            e.wait("d_x", "p2_mask")
            for h, qc, nstrip, strips in P2IT:
                idx = h * 8 + qc
                op_ = idx % 2
                if idx >= 1:
                    pidx = idx - 1
                    ph_, pqc = pidx // 8, pidx % 8
                    e.wait("s_sa", f"p2_lc_{pidx}")
                    dst = oT[(ph_ % 2) * 64:(ph_ % 2) * 64 + 64,
                             (ph_ // 2) * T + pqc * 512:(ph_ // 2) * T + (pqc + 1) * 512]
                    with _lp(nc, e.eng is not None):
                        e.op("tensor_tensor", dst, ps_o[pidx % 2][0:64, :], linvs[pidx % 2][:, :],
                             ALU.mult, sem="s_ve", n=1, label=f"p2_otev_{pidx}")
                for kt, gs, gm, masked in strips:
                    if not masked:
                        continue
                    e.wait("s_pe", f"p2_s_{gs}")
                    if gm >= 2:
                        e.wait("s_sa", f"p2_exp_{GS_OF_M[gm-2]}")
                    i_rel = kt - 4 * qc
                    e.op("tensor_tensor", stmp[gm % 2][:, :],
                         ps_s[gs % 4][:, :],
                         masks[:, 384 - 128 * i_rel: 896 - 128 * i_rel],
                         ALU.add, sem="s_ve", n=1, label=f"p2_m_{gs}")
                e.wait("s_pe", f"p2_pv_{strips[-1][1]}")
                e.op("reciprocal", linvf[:, :], ps_o[op_][64:65, :], sem="s_ve", n=1,
                     label=f"p2_rc_{idx}")
                e.wait("s_ve", f"p2_rc_{idx}")
                if idx >= 1:
                    e.wait("s_pe", f"p2_lb_{idx-1}")
                with _lp(nc, e.eng is not None):
                    e.op("tensor_copy", linvb[:, :], linvf[:, :], sem="s_ve", n=1,
                         label=f"p2_linv_{idx}")
            pidx = (P2_HEADS - 1) * 8 + (P2_QC - 1)
            ph_, pqc = pidx // 8, pidx % 8
            e.wait("s_sa", f"p2_lc_{pidx}")
            dst = oT[(ph_ % 2) * 64:(ph_ % 2) * 64 + 64,
                     (ph_ // 2) * T + pqc * 512:(ph_ // 2) * T + (pqc + 1) * 512]
            with _lp(nc, e.eng is not None):
                e.op("tensor_tensor", dst, ps_o[pidx % 2][0:64, :], linvs[pidx % 2][:, :],
                     ALU.mult, sem="s_ve", n=1, label=f"p2_otev_{pidx}")

        def p2_scalar(e):
            for h, qc, nstrip, strips in P2IT:
                idx = h * 8 + qc
                if idx >= 1:
                    pidx = idx - 1
                    e.wait("s_pe", f"p2_lb_{pidx}")
                    if pidx >= 2:
                        e.wait("s_ve", f"p2_otev_{pidx-2}")
                    e.op("copy", linvs[pidx % 2][:, :], ps_l[pidx % 2][:, :],
                         sem="s_sa", n=1, label=f"p2_lc_{pidx}")
                for kt, gs, gm, masked in strips:
                    if masked:
                        e.wait("s_ve", f"p2_m_{gs}")
                        src = stmp[gm % 2][:, :]
                    else:
                        e.wait("s_pe", f"p2_s_{gs}")
                        src = ps_s[gs % 4][:, :]
                    if gs == 0:
                        e.wait("s_gp", "c_m32")
                    with _lp(nc, e.eng is not None):
                        e.op("activation", pbuf[gs % 4][:, :], src, AF.Exp, bias=cm32[:, 0:1],
                             sem="s_sa", n=1, label=f"p2_exp_{gs}")
            pidx = (P2_HEADS - 1) * 8 + (P2_QC - 1)
            e.wait("s_pe", f"p2_lb_{pidx}")
            e.op("copy", linvs[pidx % 2][:, :], ps_l[pidx % 2][:, :],
                 sem="s_sa", n=1, label=f"p2_lc_{pidx}")

        run_phase({"sync": p2_sync, "tensor": p2_tensor, "vector": p2_vector,
                   "scalar": p2_scalar, "gpsimd": p2_gpsimd})

        if "oT" in TAPS:
            def p2_tap(e):
                e.wait("s_ve", f"p2_otev_{(P2_HEADS-1)*8 + (P2_QC-1)}")
                e.op("dma_start", tap_oT[:, :], oT[:, :], sem="d_x", n=16)
            run_phase({"sync": p2_tap})

        barrier()
        p2.close()

    if 3 in PHASES:
        p3 = ExitStack()
        wos = p3.enter_context(nc.sbuf_tensor("wos", [128, 8 * D], BF16))
        bob = p3.enter_context(nc.sbuf_tensor("bob", [128, D], F32))
        xt3 = [p3.enter_context(nc.sbuf_tensor(f"xt3{i}", [128, D], F32)) for i in range(3)]
        x1t = [p3.enter_context(nc.sbuf_tensor(f"x1t{i}", [128, D], F32)) for i in range(2)]
        h2s = [p3.enter_context(nc.sbuf_tensor(f"h2s{i}", [128, D], BF16)) for i in range(2)]
        h2Ts = [p3.enter_context(nc.sbuf_tensor(f"h2Ts{i}", [128, D], BF16)) for i in range(2)]
        sqd3 = p3.enter_context(nc.sbuf_tensor("sqd3", [128, D], BF16))
        st3 = {}
        for nm in ("ssum", "ssq", "mm_", "vv", "m2v", "var0", "lnv", "rstd", "nb"):
            st3[nm] = p3.enter_context(nc.sbuf_tensor("g_" + nm, [128, 2], F32))
        gws = p3.enter_context(nc.sbuf_tensor("gws", [8, D], F32))
        gwT = p3.enter_context(nc.sbuf_tensor("gwT", [128, 64], BF16))
        lgs = p3.enter_context(nc.sbuf_tensor("lgs", [128, NTILE * E], F32))
        b1stg = p3.enter_context(nc.sbuf_tensor("b1stg", [32, 128], F32))
        gtmp = {}
        for nm in ("m1", "se", "ri", "m1p", "m2", ):
            gtmp[nm] = p3.enter_context(nc.sbuf_tensor("t_" + nm, [128, 1], F32))
        for nm in ("ls", "pe_", "pr_", "mk", "pm", "kp"):
            gtmp[nm] = p3.enter_context(nc.sbuf_tensor("t_" + nm, [128, 8], F32))
        ps_ao = [p3.enter_context(nc.psum_tensor(f"ps_ao{i}", [128, 1024], F32)) for i in range(2)]
        pt3 = [p3.enter_context(nc.psum_tensor(f"pt3{i}", [128, 1024], BF16)) for i in range(2)]
        ps_lg = [p3.enter_context(nc.psum_tensor(f"ps_lg{i}", [128, 512], F32)) for i in range(2)]

        def p3_gpsimd(e):
            src = wo_d.rearrange("(dc p) c -> p dc c", p=128)
            dst = wos[:, :].rearrange("p (dc c) -> p dc c", c=D)
            e.op("dma_start", dst, src, sem="d_w", n=16, label="p3_wo")
            e.op("dma_start", b2s[:, :], b2_d[:, :], sem="d_w", n=16, label="p3_b2")

        def p3_sync(e):
            e.op("dma_start", bob[:, :], bass.AP(bo_d.tensor, 0, [[0, 128], [1, D]]),
                 sem="d_x", n=16, label="p3_bo")
            e.op("dma_start", gws[:, :], gw_d[:, :], sem="d_x", n=16, label="p3_gw")
            b1r = b1_d.rearrange("e (r c) -> e r c", c=128)
            for ee in range(E):
                if ee >= 1:
                    e.wait("s_pe", f"p3_b1t_{ee-1}")
                e.op("dma_start", b1stg[:, :], b1r[ee], sem="d_x", n=16, label=f"p3_b1_{ee}")
            for i in range(NTILE):
                if i >= 3:
                    e.wait("s_ve", f"p3_x1_{i-3}")
                e.op("dma_start", xt3[i % 3][:, :], x_d[i * 128:(i + 1) * 128, :],
                     sem="d_x", n=16, label=f"p3_x_{i}")

        def p3_tensor(e):
            if 2 in PHASES:
                _pl = (P2_HEADS - 1) * 8 + (P2_QC - 1)
                e.wait("s_ve", f"p2_otev_{_pl}")
            e.wait("d_x", "p3_gw")
            e.wait("d_x", "c_idf")
            for dc in range(8):
                if dc >= 2:
                    e.wait("s_sa", f"p3_gwe_{dc-2}")
                e.op("transpose", ps_lg[dc % 2][0:128, 0:8], gws[0:8, dc * 128:(dc + 1) * 128],
                     idf[0:8, 0:8], sem="s_pe", n=1, label=f"p3_gwt_{dc}")
            for ee in range(E):
                e.wait("d_x", f"p3_b1_{ee}")
                j = 8 + ee
                if ee >= 2:
                    e.wait("s_ve", f"p3_b1e_{ee-2}")
                else:
                    e.wait("s_sa", f"p3_gwe_{6+ee}")
                e.op("transpose", ps_lg[j % 2][0:128, 0:32], b1stg[:, :], idf[0:32, 0:32],
                     sem="s_pe", n=1, label=f"p3_b1t_{ee}")
            for i in range(NTILE):
                e.wait("d_w", "p3_wo")
                if i >= 2:
                    e.wait("s_ve", f"p3_x1_{i-2}")
                for dc in range(8):
                    for dh in range(2):
                        e.op("matmul", ps_ao[i % 2][:, dh * 512:(dh + 1) * 512],
                             oT[:, dc * T + i * 128: dc * T + (i + 1) * 128],
                             wos[:, dc * D + dh * 512: dc * D + (dh + 1) * 512],
                             start=(dc == 0), stop=(dc == 7),
                             sem=("s_pe" if (dc == 7 and dh == 1) else None), n=1,
                             label=(f"p3_ao_{i}" if (dc == 7 and dh == 1) else None))
                e.wait("s_sa", f"p3_norm_{i}")
                for dc in range(8):
                    j = i * 8 + dc
                    if j >= 2:
                        e.wait("s_ve", f"p3_te_{(j-2)//8}_{(j-2)%8}")
                    e.op("transpose", pt3[j % 2][:, 0:128], h2s[i % 2][:, dc * 128:(dc + 1) * 128],
                         idb[:, :], sem="s_pe", n=1, label=f"p3_tr_{i}_{dc}")
                e.wait("s_ve", f"p3_te_{i}_7")
                e.wait("s_sa", "p3_gwe_7")
                if i >= 2:
                    e.wait("s_ve", f"p3_lge_{i-2}")
                else:
                    e.wait("s_ve", f"p3_b1e_{6+i}")
                for dc in range(8):
                    e.op("matmul", ps_lg[i % 2][0:128, 0:8],
                         h2Ts[i % 2][:, dc * 128:(dc + 1) * 128], gwT[:, dc * 8:(dc + 1) * 8],
                         start=(dc == 0), stop=(dc == 7),
                         sem=("s_pe" if dc == 7 else None), n=1,
                         label=(f"p3_lg_{i}" if dc == 7 else None))
            for t in range(NTILE):
                e.wait("s_ve", f"p3_g_{t}")
                j = t % 2
                if t >= 2:
                    e.wait("s_ve", f"p3_gte_{t-2}")
                else:
                    e.wait("s_ve", f"p3_lge_{30+t}")
                e.op("transpose", ps_lg[j][0:8, 0:128], g_sb[:, t * 8:(t + 1) * 8], idf[:, :],
                     sem="s_pe", n=1, label=f"p3_gt_{t}")

        def p3_vector(e):
            inv_d = 1.0 / D
            for ee in range(E):
                e.wait("s_pe", f"p3_b1t_{ee}")
                e.op("tensor_copy", b1T[:, ee * 32:(ee + 1) * 32],
                     ps_lg[(8 + ee) % 2][0:128, 0:32], sem="s_ve", n=1, label=f"p3_b1e_{ee}")
            for i in range(NTILE):
                p = i % 2
                e.wait("s_pe", f"p3_ao_{i}")
                e.wait("d_x", f"p3_x_{i}")
                e.wait("d_x", "p3_bo")
                if i >= 2:
                    e.wait("d_so", f"p3_x1d_{i-2}")
                    e.wait("s_sa", f"p3_norm_{i-2}")
                e.op("tensor_tensor", x1t[p][:, :], ps_ao[p][:, :], bob[:, :], ALU.add,
                     sem="s_ve", n=1, label=f"p3_x1a_{i}")
                e.wait("s_ve", f"p3_x1a_{i}")
                e.op("tensor_tensor", x1t[p][:, :], x1t[p][:, :], xt3[i % 3][:, :], ALU.add,
                     sem="s_ve", n=1, label=f"p3_x1_{i}")
                e.wait("s_ve", f"p3_x1_{i}")
                e.op("tensor_reduce", st3["ssum"][:, p:p + 1], x1t[p][:, :], AX.X, ALU.add,
                     sem="s_ve", n=1, label=f"p3_sum_{i}")
                e.wait("s_ve", f"p3_sum_{i}")
                e.op("tensor_scalar", st3["mm_"][:, p:p + 1], st3["ssum"][:, p:p + 1],
                     inv_d, 0.0, ALU.mult, ALU.bypass, sem="s_ve", n=1, label=f"p3_m_{i}")
                e.wait("s_ve", f"p3_m_{i}")
                e.wait("s_sa", f"p3_sq_{i}")
                e.op("tensor_scalar", st3["vv"][:, p:p + 1], st3["ssq"][:, p:p + 1],
                     inv_d, EPS, ALU.mult, ALU.add, sem="s_ve", n=1, label=f"p3_vv_{i}")
                e.op("tensor_tensor", st3["m2v"][:, p:p + 1], st3["mm_"][:, p:p + 1],
                     st3["mm_"][:, p:p + 1], ALU.mult, sem="s_ve", n=1, label=f"p3_m2_{i}")
                e.wait("s_ve", f"p3_m2_{i}")
                e.op("tensor_tensor", st3["var0"][:, p:p + 1], st3["vv"][:, p:p + 1],
                     st3["m2v"][:, p:p + 1], ALU.subtract, sem="s_ve", n=1, label=f"p3_var_{i}")
                e.wait("s_sa", f"p3_rstd_{i}")
                e.op("tensor_scalar", st3["nb"][:, p:p + 1], st3["mm_"][:, p:p + 1],
                     st3["rstd"][:, p:p + 1], -1.0, ALU.mult, ALU.mult,
                     sem="s_ve", n=1, label=f"p3_nb_{i}")
                for dc in range(8):
                    j = i * 8 + dc
                    e.wait("s_pe", f"p3_tr_{i}_{dc}")
                    if i >= 2 and dc == 0:
                        e.wait("d_so", f"p3_h2d_{i-2}")
                        e.wait("s_pe", f"p3_lg_{i-2}")
                    e.op("tensor_copy", h2Ts[i % 2][:, dc * 128:(dc + 1) * 128],
                         pt3[j % 2][:, 0:128], sem="s_ve", n=1, label=f"p3_te_{i}_{dc}")
                e.wait("s_pe", f"p3_lg_{i}")
                e.op("tensor_copy", lgs[:, i * 8:(i + 1) * 8], ps_lg[i % 2][0:128, 0:8],
                     sem="s_ve", n=1, label=f"p3_lge_{i}")
            for t in range(NTILE):
                lg = lgs[:, t * 8:(t + 1) * 8]
                e.op("tensor_reduce", gtmp["m1"][:, :], lg, AX.X, ALU.max,
                     sem="s_ve", n=1, label=f"p3_ga_{t}")
                e.wait("s_ve", f"p3_ga_{t}")
                if t >= 1:
                    e.wait("s_sa", f"p3_ge_{t-1}")
                e.op("tensor_scalar", gtmp["ls"][:, :], lg, gtmp["m1"][:, 0:1], 0.0,
                     ALU.subtract, ALU.bypass, sem="s_ve", n=1, label=f"p3_gb_{t}")
                e.wait("s_sa", f"p3_ge_{t}")
                e.op("tensor_reduce", gtmp["se"][:, :], gtmp["pe_"][:, :], AX.X, ALU.add,
                     sem="s_ve", n=1, label=f"p3_gc_{t}")
                e.wait("s_ve", f"p3_gc_{t}")
                e.op("reciprocal", gtmp["ri"][:, :], gtmp["se"][:, :],
                     sem="s_ve", n=1, label=f"p3_gd_{t}")
                e.wait("s_ve", f"p3_gd_{t}")
                e.op("tensor_scalar", gtmp["pr_"][:, :], gtmp["pe_"][:, :],
                     gtmp["ri"][:, 0:1], 0.0, ALU.mult, ALU.bypass,
                     sem="s_ve", n=1, label=f"p3_gf_{t}")
                e.wait("s_ve", f"p3_gf_{t}")
                e.op("tensor_reduce", gtmp["m1p"][:, :], gtmp["pr_"][:, :], AX.X, ALU.max,
                     sem="s_ve", n=1, label=f"p3_gg_{t}")
                e.wait("s_ve", f"p3_gg_{t}")
                e.op("tensor_scalar", gtmp["mk"][:, :], gtmp["pr_"][:, :],
                     gtmp["m1p"][:, 0:1], NEG, ALU.is_ge, ALU.mult,
                     sem="s_ve", n=1, label=f"p3_gh_{t}")
                e.wait("s_ve", f"p3_gh_{t}")
                e.op("tensor_tensor", gtmp["pm"][:, :], gtmp["pr_"][:, :], gtmp["mk"][:, :],
                     ALU.add, sem="s_ve", n=1, label=f"p3_gi_{t}")
                e.wait("s_ve", f"p3_gi_{t}")
                e.op("tensor_reduce", gtmp["m2"][:, :], gtmp["pm"][:, :], AX.X, ALU.max,
                     sem="s_ve", n=1, label=f"p3_gj_{t}")
                e.wait("s_ve", f"p3_gj_{t}")
                e.op("tensor_scalar", gtmp["kp"][:, :], gtmp["pr_"][:, :],
                     gtmp["m2"][:, 0:1], 0.0, ALU.is_ge, ALU.bypass,
                     sem="s_ve", n=1, label=f"p3_gk_{t}")
                e.wait("s_ve", f"p3_gk_{t}")
                e.op("tensor_tensor", g_sb[:, t * 8:(t + 1) * 8], gtmp["pr_"][:, :],
                     gtmp["kp"][:, :], ALU.mult, sem="s_ve", n=1, label=f"p3_g_{t}")
            for t in range(NTILE):
                e.wait("s_pe", f"p3_gt_{t}")
                with _lp(nc, e.eng is not None):
                    e.op("tensor_copy", GT[0:8, t * 128:(t + 1) * 128],
                         ps_lg[t % 2][0:8, 0:128], sem="s_ve", n=1, label=f"p3_gte_{t}")

        def p3_scalar(e):
            for dc in range(8):
                e.wait("s_pe", f"p3_gwt_{dc}")
                with _lp(nc, e.eng is not None):
                    e.op("copy", gwT[:, dc * 8:(dc + 1) * 8], ps_lg[dc % 2][0:128, 0:8],
                         sem="s_sa", n=1, label=f"p3_gwe_{dc}")
            for i in range(NTILE):
                p = i % 2
                e.wait("s_ve", f"p3_x1_{i}")
                if i >= 2:
                    e.wait("s_ve", f"p3_vv_{i-2}")
                with _lp(nc, e.eng is not None):
                    e.op("activation", sqd3[:, :], x1t[p][:, :], AF.Square,
                         accum_out=st3["ssq"][:, p:p + 1], sem="s_sa", n=1, label=f"p3_sq_{i}")
                e.op("dma_start", x1sp[i * 128:(i + 1) * 128, :], x1t[p][:, :],
                     sem="d_so", n=16, label=f"p3_x1d_{i}")
                e.wait("s_ve", f"p3_var_{i}")
                e.op("activation", st3["lnv"][:, p:p + 1], st3["var0"][:, p:p + 1], AF.Ln,
                     sem="s_sa", n=1, label=f"p3_ln_{i}")
                e.wait("s_sa", f"p3_ln_{i}")
                e.op("activation", st3["rstd"][:, p:p + 1], st3["lnv"][:, p:p + 1], AF.Exp,
                     scale=-0.5, sem="s_sa", n=1, label=f"p3_rstd_{i}")
                e.wait("s_sa", f"p3_rstd_{i}")
                e.wait("s_ve", f"p3_nb_{i}")
                if i >= 2:
                    e.wait("s_pe", f"p3_tr_{i-2}_7")
                with _lp(nc, e.eng is not None):
                    e.op("activation", h2s[p][:, :], x1t[p][:, :], AF.Identity,
                         bias=st3["nb"][:, p:p + 1], scale=st3["rstd"][:, p:p + 1],
                         sem="s_sa", n=1, label=f"p3_norm_{i}")
                e.wait("s_ve", f"p3_te_{i}_7")
                dst = h2sp.rearrange("(dc p) t -> p dc t", p=128)[:, :, i * 128:(i + 1) * 128]
                src = h2Ts[i % 2][:, :].rearrange("p (dc t) -> p dc t", t=128)
                e.op("dma_start", dst, src, sem="d_so", n=16, label=f"p3_h2d_{i}")
            for t in range(NTILE):
                e.wait("s_ve", f"p3_gb_{t}")
                if t >= 1:
                    e.wait("s_ve", f"p3_gf_{t-1}")
                e.op("activation", gtmp["pe_"][:, :], gtmp["ls"][:, :], AF.Exp,
                     sem="s_sa", n=1, label=f"p3_ge_{t}")
            e.wait("d_so", f"p3_h2d_{NTILE-1}")
            e.wait("d_so", f"p3_x1d_{NTILE-1}")

        run_phase({"gpsimd": p3_gpsimd, "sync": p3_sync, "tensor": p3_tensor,
                   "vector": p3_vector, "scalar": p3_scalar})

        if "g" in TAPS:
            def p3_tap(e):
                e.wait("s_ve", f"p3_g_{NTILE-1}")
                e.op("dma_start", tap_g[:, :], g_sb[:, :], sem="d_x", n=16)
            run_phase({"sync": p3_tap})

        barrier()
        p3.close()

    es_r.close()

    if 4 in PHASES:
        p4 = ExitStack()
        h2q = p4.enter_context(nc.sbuf_tensor("h2q", [128, 8 * TQ], BF16))
        acc = p4.enter_context(nc.sbuf_tensor("acc", [128, 8 * TQ], F32))
        hid = p4.enter_context(nc.sbuf_tensor("hid", [128, 32 * TQ], BF16))
        w2s = p4.enter_context(nc.sbuf_tensor("w2s", [128, 32 * D], BF16))
        w1r = [p4.enter_context(nc.sbuf_tensor(f"w1r{i}", [128, 8 * 256], BF16)) for i in range(2)]
        x1t4 = [p4.enter_context(nc.sbuf_tensor(f"x1t4{i}", [128, D], F32)) for i in range(2)]
        ph4 = [p4.enter_context(nc.psum_tensor(f"ph4{i}", [128, 1024], F32)) for i in range(2)]
        po4 = [p4.enter_context(nc.psum_tensor(f"po4{i}", [128, 1024], F32)) for i in range(2)]

        NCH = 16

        def p4_sync(e):
            e.wait("d_w", "pc_w_255")
            for q in range(NQ):
                if q >= 1:
                    e.wait("s_pe", f"p4_mm1f_{q*8-1}_31")
                src = h2sp.rearrange("(dc p) t -> p dc t", p=128)[:, :, q * TQ:(q + 1) * TQ]
                dst = h2q[:, :].rearrange("p (dc t) -> p dc t", t=TQ)
                e.op("dma_start", dst, src, sem="d_x", n=16, label=f"p4_h2_{q}")
                for ee in range(E):
                    ge = q * 8 + ee
                    for c in range(NCH):
                        cg = ge * NCH + c
                        if cg >= 2:
                            pge, pc = (cg - 2) // NCH, (cg - 2) % NCH
                            e.wait("s_pe", f"p4_mm1f_{pge}_{2*pc+1}")
                        src = bass.AP(w1b.tensor, ee * D * FF + c * 256,
                                      [[FF, 128], [128 * FF, 8], [1, 256]])
                        dst = w1r[cg % 2][:, :].rearrange("p (dc f) -> p dc f", f=256)
                        e.op("dma_start", dst, src, sem="d_x", n=16, label=f"p4_w1_{cg}")
                    if ge >= 1:
                        e.wait("s_pe", f"p4_mm2t_{ge-1}_7")
                    src = bass.AP(w2b.tensor, ee * FF * D,
                                  [[D, 128], [128 * D, 32], [1, D]])
                    dst = w2s[:, :].rearrange("p (fb d) -> p fb d", d=D)
                    e.op("dma_start", dst, src, sem="d_x", n=16, label=f"p4_w2_{ge}")
                for tt in range(8):
                    gt = q * 8 + tt
                    if gt >= 2:
                        e.wait("d_so", f"p4_yd_{(gt-2)//8}_{(gt-2)%8}")
                    e.op("dma_start", x1t4[tt % 2][:, :],
                         x1sp[gt * 128:(gt + 1) * 128, :], sem="d_x", n=16,
                         label=f"p4_x1_{q}_{tt}")

        def p4_tensor(e):
            for q in range(NQ):
                for ee in range(E):
                    ge = q * 8 + ee
                    e.wait("d_x", f"p4_h2_{q}")
                    for c in range(NCH):
                        cg = ge * NCH + c
                        e.wait("d_x", f"p4_w1_{cg}")
                        for fbl in range(2):
                            fb = 2 * c + fbl
                            if fb >= 2:
                                e.wait("s_sa", f"p4_relu_{ge}_{fb-2}")
                            elif ge >= 1:
                                e.wait("s_sa", f"p4_relu_{ge-1}_{fb+30}")
                            for dc in range(8):
                                for th in range(2):
                                    e.op("matmul", ph4[fb % 2][:, th * 512:(th + 1) * 512],
                                         w1r[cg % 2][:, dc * 256 + fbl * 128: dc * 256 + fbl * 128 + 128],
                                         h2q[:, dc * TQ + th * 512: dc * TQ + (th + 1) * 512],
                                         start=(dc == 0), stop=(dc == 7),
                                         sem=("s_pe" if (dc == 7 and th == 1) else None), n=1,
                                         label=(f"p4_mm1f_{ge}_{fb}" if (dc == 7 and th == 1) else None))
                    e.wait("d_x", f"p4_w2_{ge}")
                    e.wait("s_sa", f"p4_relu_{ge}_31")
                    for tt in range(8):
                        if tt >= 2:
                            e.wait("s_ve", f"p4_acc_{ge}_{tt-2}")
                        elif ge % 8 == 0 and ge >= 1:
                            e.wait("s_ve", f"p4_cb_{ge//8-1}_{tt+6}")
                        elif ge >= 1:
                            e.wait("s_ve", f"p4_acc_{ge-1}_{tt+6}")
                        for fb in range(32):
                            for dh in range(2):
                                e.op("matmul", po4[tt % 2][:, dh * 512:(dh + 1) * 512],
                                     hid[:, fb * TQ + tt * 128: fb * TQ + (tt + 1) * 128],
                                     w2s[:, fb * D + dh * 512: fb * D + (dh + 1) * 512],
                                     start=(fb == 0), stop=(fb == 31),
                                     sem=("s_pe" if (fb == 31 and dh == 1) else None), n=1,
                                     label=(f"p4_mm2t_{ge}_{tt}" if (fb == 31 and dh == 1) else None))
                if q == 0:
                    e.wait("s_ve", f"p3_gte_{NTILE-1}")
                    e.wait("d_w", "p3_b2")
                for tt in range(8):
                    ge = q * 8 + 7
                    if tt >= 2:
                        e.wait("s_ve", f"p4_cb_{q}_{tt-2}")
                    else:
                        e.wait("s_ve", f"p4_acc_{ge}_{tt+6}")
                    e.wait("s_ve", f"p4_acc_{ge}_{tt}")
                    for dh in range(2):
                        e.op("matmul", po4[tt % 2][:, dh * 512:(dh + 1) * 512],
                             GT[0:8, (q * 8 + tt) * 128:(q * 8 + tt + 1) * 128],
                             b2s[0:8, dh * 512:(dh + 1) * 512],
                             start=True, stop=True,
                             sem=("s_pe" if dh == 1 else None), n=1,
                             label=(f"p4_fc_{q}_{tt}" if dh == 1 else None))

        def p4_scalar(e):
            e.wait("s_ve", f"p3_g_{NTILE-1}")
            e.wait("s_ve", f"p3_b1e_{E-1}")
            for q in range(NQ):
                for ee in range(E):
                    ge = q * 8 + ee
                    for fb in range(32):
                        e.wait("s_pe", f"p4_mm1f_{ge}_{fb}")
                        with _lp(nc, e.eng is not None):
                            e.op("activation", hid[:, fb * TQ:(fb + 1) * TQ], ph4[fb % 2][:, :],
                                 AF.Relu, bias=b1T[:, ee * 32 + fb: ee * 32 + fb + 1],
                                 sem="s_sa", n=1, label=f"p4_relu_{ge}_{fb}")
                    for tt in range(8):
                        e.wait("s_pe", f"p4_mm2t_{ge}_{tt}")
                        e.op("activation", po4[tt % 2][:, :], po4[tt % 2][:, :], AF.Copy,
                             scale=g_sb[:, (q * 8 + tt) * 8 + ee: (q * 8 + tt) * 8 + ee + 1],
                             sem="s_sa", n=1, label=f"p4_sc_{ge}_{tt}")
                for tt in range(8):
                    e.wait("s_ve", f"p4_y_{q}_{tt}")
                    e.op("dma_start", y_d[(q * 8 + tt) * 128:(q * 8 + tt + 1) * 128, :],
                         x1t4[tt % 2][:, :], sem="d_so", n=16, label=f"p4_yd_{q}_{tt}")
            e.wait("d_so", f"p4_yd_{NQ-1}_7")

        def p4_vector(e):
            for q in range(NQ):
                e.op("memset", acc[:, :], 0.0, sem="s_ve", n=1, label=f"p4_ms_{q}")
                for ee in range(E):
                    ge = q * 8 + ee
                    for tt in range(8):
                        e.wait("s_sa", f"p4_sc_{ge}_{tt}")
                        if ee == 0:
                            e.wait("s_ve", f"p4_ms_{q}")
                        e.op("tensor_tensor", acc[:, tt * D:(tt + 1) * D],
                             po4[tt % 2][:, :], acc[:, tt * D:(tt + 1) * D], ALU.add,
                             sem="s_ve", n=1, label=f"p4_acc_{ge}_{tt}")
                for tt in range(8):
                    e.wait("s_pe", f"p4_fc_{q}_{tt}")
                    e.wait("d_x", f"p4_x1_{q}_{tt}")
                    e.op("tensor_tensor", x1t4[tt % 2][:, :], po4[tt % 2][:, :],
                         x1t4[tt % 2][:, :], ALU.add, sem="s_ve", n=1, label=f"p4_cb_{q}_{tt}")
                    e.wait("s_ve", f"p4_cb_{q}_{tt}")
                    e.op("tensor_tensor", x1t4[tt % 2][:, :], x1t4[tt % 2][:, :],
                         acc[:, tt * D:(tt + 1) * D], ALU.add,
                         sem="s_ve", n=1, label=f"p4_y_{q}_{tt}")

        run_phase({"sync": p4_sync, "tensor": p4_tensor,
                   "scalar": p4_scalar, "vector": p4_vector})
        p4.close()

    es.close()
    return nc


_NC_CACHE = {}


def _get_nc():
    key = (tuple(sorted(TAPS)), tuple(PHASES))
    if key not in _NC_CACHE:
        _NC_CACHE[key] = build_nc()
    return _NC_CACHE[key]


def _consts():
    bf = ml_dtypes.bfloat16
    idb = np.eye(128, dtype=np.float32).astype(bf)
    idf = np.eye(128, dtype=np.float32)
    p = np.arange(128)[:, None]
    c = np.arange(896)[None, :]
    maskc = np.where(p <= c - 384, 0.0, NEG).astype(np.float32)
    onesb = np.ones((1, 128), dtype=np.float32).astype(bf)
    return {"idb": idb, "idf": idf, "maskc": maskc, "onesb": onesb}


def kernel(L0_attn_x, L0_attn_Wq, L0_attn_Wk, L0_attn_Wv, L0_attn_Wo, L0_attn_bo,
           L0_ffn_gate_w, L0_ffn_w1, L0_ffn_b1, L0_ffn_w2, L0_ffn_b2):
    nc = _get_nc()
    consts = _consts()
    base = {
        "wq": np.ascontiguousarray(L0_attn_Wq, dtype=np.float32),
        "wk": np.ascontiguousarray(L0_attn_Wk, dtype=np.float32),
        "wv": np.ascontiguousarray(L0_attn_Wv, dtype=np.float32),
        "wo": np.ascontiguousarray(L0_attn_Wo, dtype=np.float32),
        "bo": np.ascontiguousarray(L0_attn_bo, dtype=np.float32).reshape(1, D),
        "gw": np.ascontiguousarray(L0_ffn_gate_w, dtype=np.float32),
        "w1": np.ascontiguousarray(L0_ffn_w1, dtype=np.float32),
        "b1": np.ascontiguousarray(L0_ffn_b1, dtype=np.float32),
        "w2": np.ascontiguousarray(L0_ffn_w2, dtype=np.float32),
        "b2": np.ascontiguousarray(L0_ffn_b2, dtype=np.float32),
    }
    base.update(consts)
    xs = np.asarray(L0_attn_x, dtype=np.float32)
    in_maps = []
    for b in range(8):
        m = dict(base)
        m["x"] = np.ascontiguousarray(xs[b])
        in_maps.append(m)
    res = run_bass_kernel_spmd(nc, in_maps, core_ids=list(range(8)))
    out = np.stack([res.results[b]["y"] for b in range(8)], axis=0)
    return out.astype(np.float32)


def run_debug(inputs, taps, phases=(1, 2, 3, 4), cores=(0,), p2_heads=None, p2_qc=None,
              p2_dbg=None):
    global TAPS, PHASES, P2_HEADS, P2_QC, P2_DBG
    if p2_dbg is not None:
        P2_DBG = p2_dbg
    TAPS = set(taps)
    PHASES = tuple(phases)
    if p2_heads is not None:
        P2_HEADS = p2_heads
    if p2_qc is not None:
        P2_QC = p2_qc
    _NC_CACHE.clear()
    nc = _get_nc()
    consts = _consts()
    base = {
        "wq": np.ascontiguousarray(inputs["L0_attn_Wq"], dtype=np.float32),
        "wk": np.ascontiguousarray(inputs["L0_attn_Wk"], dtype=np.float32),
        "wv": np.ascontiguousarray(inputs["L0_attn_Wv"], dtype=np.float32),
        "wo": np.ascontiguousarray(inputs["L0_attn_Wo"], dtype=np.float32),
        "bo": np.ascontiguousarray(inputs["L0_attn_bo"], dtype=np.float32).reshape(1, D),
        "gw": np.ascontiguousarray(inputs["L0_ffn_gate_w"], dtype=np.float32),
        "w1": np.ascontiguousarray(inputs["L0_ffn_w1"], dtype=np.float32),
        "b1": np.ascontiguousarray(inputs["L0_ffn_b1"], dtype=np.float32),
        "w2": np.ascontiguousarray(inputs["L0_ffn_w2"], dtype=np.float32),
        "b2": np.ascontiguousarray(inputs["L0_ffn_b2"], dtype=np.float32),
    }
    base.update(consts)
    xs = np.asarray(inputs["L0_attn_x"], dtype=np.float32)
    in_maps = []
    for b in cores:
        m = dict(base)
        m["x"] = np.ascontiguousarray(xs[b])
        in_maps.append(m)
    res = run_bass_kernel_spmd(nc, in_maps, core_ids=list(cores))
    TAPS = set()
    PHASES = (1, 2, 3, 4)
    P2_HEADS = H
    P2_QC = 8
    _NC_CACHE.clear()
    return res.results
